# version 03 (final)
# baseline (speedup 1.0000x reference)
import contextlib
import numpy as np
import ml_dtypes
import concourse.bass as bass
import concourse.mybir as mybir
from concourse.bass_utils import run_bass_kernel_spmd

F32 = mybir.dt.float32
BF16 = mybir.dt.bfloat16
I32 = mybir.dt.int32
OP = mybir.AluOpType
AX = mybir.AxisListType
AF = mybir.ActivationFunctionType
BF = ml_dtypes.bfloat16

D = 4096
H = 32
DH = 128
NT = 8
SCALE = float(DH) ** -0.5
LN_EPS = 1e-5
CLIP = 8.0

def alibi_slopes():
    m = np.arange(1, 33, dtype=np.float64) * (8.0 / 32.0)
    return 1.0 / (2.0 ** m)

SLOPES = alibi_slopes()



def serializer(eng, sem):
    state = {"n": 0}
    def ser(inst):
        state["n"] += 1
        inst.then_inc(sem, 1)
        eng.wait_ge(sem, state["n"])
        return inst
    return ser


def emit_ln_lnT(nc, sems, tag, x_dram_tiles, lnT, ident, epst=None, nt=NT):
    with contextlib.ExitStack() as ctx:
        xt = [ctx.enter_context(nc.sbuf_tensor(f"{tag}xt{i}", [128, D], F32)) for i in range(2)]
        lnb = ctx.enter_context(nc.sbuf_tensor(f"{tag}lnb", [128, D], BF16))
        dump = ctx.enter_context(nc.sbuf_tensor(f"{tag}dump", [128, D], BF16))
        st = ctx.enter_context(nc.sbuf_tensor(f"{tag}st", [128, 8], F32))
        ps = [ctx.enter_context(nc.psum_tensor(f"{tag}ps{i}", [128, 512], BF16)) for i in range(2)]
        xload, ssd, varr, rstdd, lnbr, tpd, evd, sser, vser = sems(9)

        with nc.Block() as block:
            @block.sync
            def _(s):
                for t in range(nt):
                    if t >= 2:
                        s.wait_ge(lnbr, t - 1)
                    s.dma_start(out=xt[t % 2][:], in_=x_dram_tiles[t]).then_inc(xload, 16)

            @block.scalar
            def _(s):
                ser = serializer(s, sser)
                for t in range(nt):
                    s.wait_ge(xload, 16 * (t + 1))
                    if t >= 1:
                        s.wait_ge(varr, t)
                    s.activation(dump[:], xt[t % 2][:], AF.Square,
                                 accum_out=st[:, 2:3]).then_inc(ssd, 1)
                    s.wait_ge(varr, t + 1)
                    ser(s.activation(st[:, 7:8], st[:, 5:6], AF.Ln, bias=epst[:, 0:1]))
                    s.activation(st[:, 6:7], st[:, 7:8], AF.Exp, scale=-0.5).then_inc(rstdd, 1)

            @block.vector
            def _(v):
                ser = serializer(v, vser)
                for t in range(nt):
                    x = xt[t % 2]
                    v.wait_ge(xload, 16 * (t + 1))
                    ser(v.reduce_sum(st[:, 0:1], x[:], axis=AX.X))
                    ser(v.tensor_scalar(st[:, 1:2], st[:, 0:1], -1.0 / D, None, OP.mult))
                    v.wait_ge(ssd, t + 1)
                    v.tensor_scalar(st[:, 3:4], st[:, 2:3], 1.0 / D, None, OP.mult)
                    ser(v.tensor_tensor(st[:, 4:5], st[:, 1:2], st[:, 1:2], OP.mult))
                    v.tensor_tensor(st[:, 5:6], st[:, 3:4], st[:, 4:5],
                                    OP.subtract).then_inc(varr, 1)
                    v.wait_ge(rstdd, t + 1)
                    v.tensor_scalar(lnb[:], x[:], st[:, 1:2], st[:, 6:7],
                                    OP.add, OP.mult).then_inc(lnbr, 1)
                    for g in range(8):
                        v.wait_ge(tpd, 8 * t + g + 1)
                        v.tensor_copy(lnT[:, 4 * g:4 * (g + 1), 128 * t:128 * (t + 1)],
                                      ps[g % 2][:].rearrange("p (a b) -> p a b", a=4)
                                      ).then_inc(evd, 1)

            @block.tensor
            def _(te):
                for t in range(nt):
                    te.wait_ge(lnbr, t + 1)
                    for g in range(8):
                        gi = 8 * t + g
                        if gi >= 2:
                            te.wait_ge(evd, gi - 1)
                        inst = None
                        for jj in range(4):
                            j = 4 * g + jj
                            inst = te.transpose(ps[g % 2][:, 128 * jj:128 * (jj + 1)],
                                                lnb[:, 128 * j:128 * (j + 1)], ident[:])
                        inst.then_inc(tpd, 1)


def emit_proj_phase(nc, sems, tag, lhsT_sb, wt, NCH, CH, dst_of, clip,
                    extra_mm=None, evac_extra=None, tiles=None):
    if tiles is None:
        tiles = list(range(NT))
    nt = len(tiles)
    """One streaming projection Block.
      out[t, ch] = sum_d lhsT_sb[:,d,tslice].T @ bf16(wt[ch,d,:,:]) -> dst_of(ch,t)
    extra_mm(te, ch, t, pb): optional additional accumulating matmul (start=False).
    evac_extra(v, ch, t, stage_ap): optional extra vector op after clip/copy.
    """
    with contextlib.ExitStack() as ctx:
        wbuf = [ctx.enter_context(nc.sbuf_tensor(f"{tag}wb{i}", [128, 32, CH], BF16)) for i in range(2)]
        ps = [ctx.enter_context(nc.psum_tensor(f"{tag}ps{i}", [128, CH], F32)) for i in range(2)]
        stage = [ctx.enter_context(nc.sbuf_tensor(f"{tag}stg{i}", [128, CH], BF16)) for i in range(2)]
        wload, mmd, evd, outd, pvser = sems(5)

        with nc.Block() as block:
            @block.gpsimd
            def _(g):
                for ch in range(NCH):
                    if ch >= 2:
                        g.wait_ge(mmd, nt * (ch - 1))
                    g.dma_start(out=wbuf[ch % 2][:, :16],
                                in_=wt.ap()[ch, :16].rearrange("a p c -> p a c")).then_inc(wload, 16)
                    g.dma_start(out=wbuf[ch % 2][:, 16:],
                                in_=wt.ap()[ch, 16:].rearrange("a p c -> p a c")).then_inc(wload, 16)

            @block.tensor
            def _(te):
                for ch in range(NCH):
                    te.wait_ge(wload, 32 * (ch + 1))
                    for ti, t in enumerate(tiles):
                        idx = ch * nt + ti
                        if idx >= 2:
                            te.wait_ge(evd, idx - 1)
                        pb = ps[idx % 2]
                        for d in range(32):
                            inst = te.matmul(pb[:], lhsT_sb[:, d, 128 * t:128 * (t + 1)],
                                             wbuf[ch % 2][:, d, :],
                                             start=(d == 0), stop=(d == 31 and extra_mm is None))
                        if extra_mm is not None:
                            inst = extra_mm(te, ch, t, pb)
                        inst.then_inc(mmd, 1)

            @block.vector
            def _(v):
                ser = serializer(v, pvser)
                for ch in range(NCH):
                    for ti, t in enumerate(tiles):
                        idx = ch * nt + ti
                        v.wait_ge(mmd, idx + 1)
                        if idx >= 2:
                            v.wait_ge(outd, 16 * (idx - 1))
                        if clip:
                            inst = v.tensor_scalar(stage[idx % 2][:], ps[idx % 2][:],
                                                   -CLIP, CLIP, OP.max, OP.min)
                        else:
                            inst = v.tensor_copy(stage[idx % 2][:], ps[idx % 2][:])
                        if evac_extra is not None:
                            ser(inst)
                            inst = evac_extra(v, ch, t, stage[idx % 2])
                        inst.then_inc(evd, 1)

            @block.sync
            def _(s):
                for ch in range(NCH):
                    for ti, t in enumerate(tiles):
                        idx = ch * nt + ti
                        s.wait_ge(evd, idx + 1)
                        s.dma_start(out=dst_of(ch, ti), in_=stage[idx % 2][:]).then_inc(outd, 16)
                s.wait_ge(outd, 16 * NCH * nt)


def emit_p2b_pass(nc, sems, tag, src_nat, dstT, ident, do_ln, lnw_ap, lnb_ap,
                  kind, qn2_o=None, km2_o=None, epst=None, nt=NT, out_dma_cb=None,
                  shared_out=None):
    with contextlib.ExitStack() as ctx:
        qT_sb = ctx.enter_context(nc.sbuf_tensor(f"{tag}qTsb", [128, 32, nt * 128], BF16))
        qt = [ctx.enter_context(nc.sbuf_tensor(f"{tag}qt{i}", [128, D], BF16)) for i in range(2)]
        t1 = ctx.enter_context(nc.sbuf_tensor(f"{tag}t1", [128, D], F32))
        qln = ctx.enter_context(nc.sbuf_tensor(f"{tag}qln", [128, D], BF16))
        sqs = ctx.enter_context(nc.sbuf_tensor(f"{tag}sqs", [128, D], BF16))
        dump = sqs
        nrm = ctx.enter_context(nc.sbuf_tensor(f"{tag}nrm", [128, nt, H], F32))
        st = ctx.enter_context(nc.sbuf_tensor(f"{tag}st", [128, 8], F32))
        ps = [ctx.enter_context(nc.psum_tensor(f"{tag}ps{i}", [128, 512], BF16)) for i in range(2)]
        qload, ssd, varr, rstdd, lnr, sqr2, tpd, evd, find, sser, vser = sems(11)

        with nc.Block() as block:
            @block.sync
            def _(s):
                for t in range(nt):
                    if t >= 2:
                        s.wait_ge(lnr, t - 1)
                    s.dma_start(out=qt[t % 2][:], in_=src_nat.ap()[t]).then_inc(qload, 16)

            @block.scalar
            def _(s):
                ser = serializer(s, sser)
                for t in range(nt):
                    if do_ln:
                        s.wait_ge(qload, 16 * (t + 1))
                        if t >= 1:
                            s.wait_ge(varr, t)
                        s.activation(dump[:], qt[t % 2][:], AF.Square,
                                     accum_out=st[:, 2:3]).then_inc(ssd, 1)
                        s.wait_ge(varr, t + 1)
                        ser(s.activation(st[:, 7:8], st[:, 5:6], AF.Ln, bias=epst[:, 0:1]))
                        s.activation(st[:, 6:7], st[:, 7:8], AF.Exp, scale=-0.5).then_inc(rstdd, 1)
                    s.wait_ge(lnr, t + 1)
                    s.activation(sqs[:], qln[:], AF.Square).then_inc(sqr2, 1)

            @block.vector
            def _(v):
                ser = serializer(v, vser)
                for t in range(nt):
                    q = qt[t % 2]
                    v.wait_ge(qload, 16 * (t + 1))
                    if do_ln:
                        ser(v.reduce_sum(st[:, 0:1], q[:], axis=AX.X))
                        ser(v.tensor_scalar(st[:, 1:2], st[:, 0:1], -1.0 / D, None, OP.mult))
                        v.wait_ge(ssd, t + 1)
                        v.tensor_scalar(st[:, 3:4], st[:, 2:3], 1.0 / D, None, OP.mult)
                        ser(v.tensor_tensor(st[:, 4:5], st[:, 1:2], st[:, 1:2], OP.mult))
                        v.tensor_tensor(st[:, 5:6], st[:, 3:4], st[:, 4:5],
                                        OP.subtract).then_inc(varr, 1)
                        v.wait_ge(rstdd, t + 1)
                        ser(v.tensor_scalar(t1[:], q[:], st[:, 1:2], st[:, 6:7], OP.add, OP.mult))
                        ser(v.tensor_tensor(t1[:], t1[:], lnw_ap, OP.mult))
                        v.tensor_tensor(qln[:], t1[:], lnb_ap, OP.add).then_inc(lnr, 1)
                    else:
                        v.tensor_copy(qln[:], q[:]).then_inc(lnr, 1)
                    v.wait_ge(sqr2, t + 1)
                    v.reduce_sum(nrm[:, t, :], sqs.ap().rearrange("p (h d) -> p h d", h=H),
                                 axis=AX.X)
                    for g in range(8):
                        v.wait_ge(tpd, 8 * t + g + 1)
                        v.tensor_copy(qT_sb[:, 4 * g:4 * (g + 1), 128 * t:128 * (t + 1)],
                                      ps[g % 2][:].rearrange("p (a b) -> p a b", a=4)
                                      ).then_inc(evd, 1)

            @block.tensor
            def _(te):
                for t in range(nt):
                    te.wait_ge(lnr, t + 1)
                    for g in range(8):
                        gi = 8 * t + g
                        if gi >= 2:
                            te.wait_ge(evd, gi - 1)
                        inst = None
                        for jj in range(4):
                            j = 4 * g + jj
                            inst = te.transpose(ps[g % 2][:, 128 * jj:128 * (jj + 1)],
                                                qln[:, 128 * j:128 * (j + 1)], ident[:])
                        inst.then_inc(tpd, 1)

        with nc.Block() as block:
            if kind == "ksh":
                kT_sh = shared_out["kT_sh"]
                km_sh = shared_out["km_sh"]
                myoff_sb = shared_out["myoff_sb"]
                pmax = ctx.enter_context(nc.sbuf_tensor(f"{tag}pmax", [1, nt * H], F32))
                kmxt = ctx.enter_context(nc.sbuf_tensor(f"{tag}kmxt", [1, H], F32))

                @block.gpsimd
                def _(g):
                    with g.register(f"{tag}ph") as ph, g.register(f"{tag}km") as rkm:
                        g.reg_load(ph, myoff_sb[:, 1:2])
                        g.dma_start(
                            out=bass.AP(kT_sh, ph, [[2048, 128], [262144, 32], [1, 2048]]),
                            in_=qT_sb[:]).then_inc(find, 16)
                        g.wait_ge(find, 16)
                        g.tensor_reduce(pmax[:], nrm.ap().rearrange("p a b -> p (a b)"),
                                        axis=AX.C, op=OP.max).then_inc(find, 1)

                @block.vector
                def _(v):
                    v.wait_ge(find, 17)
                    v.reduce_max(kmxt[:], pmax[:].rearrange("p (a b) -> p b a", a=nt),
                                 axis=AX.X).then_inc(find, 1)

                @block.gpsimd
                def _(g):
                    with g.register(f"{tag}km2") as rkm2:
                        g.wait_ge(find, 18)
                        g.reg_load(rkm2, myoff_sb[:, 2:3])
                        g.dma_start(out=bass.AP(km_sh, rkm2, [[1, 32]]),
                                    in_=kmxt[:]).then_inc(find, 16)
                        g.wait_ge(find, 34)
                return

            @block.sync
            def _(s):
                for gd in range(4):
                    s.dma_start(
                        out=dstT.ap()[8 * gd:8 * (gd + 1)].rearrange("a p i -> p a i"),
                        in_=qT_sb[:, 8 * gd:8 * (gd + 1), :]).then_inc(find, 16)
                s.wait_ge(find, 64)

            if kind == "q":
                @block.gpsimd
                def _(g):
                    g.dma_start(out=qn2_o.ap(),
                                in_=nrm.ap().rearrange("p a b -> p (a b)")).then_inc(find, 16)
                    g.wait_ge(find, 80)
            else:
                pmax = ctx.enter_context(nc.sbuf_tensor(f"{tag}pmax", [1, nt * H], F32))
                kmxt = ctx.enter_context(nc.sbuf_tensor(f"{tag}kmxt", [1, H], F32))

                @block.gpsimd
                def _(g):
                    g.tensor_reduce(pmax[:], nrm.ap().rearrange("p a b -> p (a b)"),
                                    axis=AX.C, op=OP.max).then_inc(find, 1)

                @block.vector
                def _(v):
                    v.wait_ge(find, 65)
                    v.reduce_max(kmxt[:], pmax[:].rearrange("p (a b) -> p b a", a=nt),
                                 axis=AX.X).then_inc(find, 1)

                @block.scalar
                def _(s):
                    s.wait_ge(find, 66)
                    s.dma_start(out=km2_o.ap(), in_=kmxt[:]).then_inc(find, 16)
                    s.wait_ge(find, 82)


def emit_ln_residual(nc, sems, tag, y_dram_tiles, x_dram_tiles, out_dram_tiles, epst=None, nt=NT):
    with contextlib.ExitStack() as ctx:
        yt = [ctx.enter_context(nc.sbuf_tensor(f"{tag}yt{i}", [128, D], BF16)) for i in range(2)]
        xt = [ctx.enter_context(nc.sbuf_tensor(f"{tag}xt{i}", [128, D], F32)) for i in range(2)]
        t1 = [ctx.enter_context(nc.sbuf_tensor(f"{tag}t1{i}", [128, D], F32)) for i in range(2)]
        dump = ctx.enter_context(nc.sbuf_tensor(f"{tag}dump", [128, D], BF16))
        st = ctx.enter_context(nc.sbuf_tensor(f"{tag}st", [128, 8], F32))
        yload, xload, ssd, varr, rstdd, resd, outd, sser, vser = sems(9)

        with nc.Block() as block:
            @block.sync
            def _(s):
                for t in range(nt):
                    if t >= 2:
                        s.wait_ge(resd, t - 1)
                    s.dma_start(out=yt[t % 2][:], in_=y_dram_tiles[t]).then_inc(yload, 16)
                    s.dma_start(out=xt[t % 2][:], in_=x_dram_tiles[t]).then_inc(xload, 16)

            @block.scalar
            def _(s):
                ser = serializer(s, sser)
                for t in range(nt):
                    s.wait_ge(yload, 16 * (t + 1))
                    if t >= 1:
                        s.wait_ge(varr, t)
                    s.activation(dump[:], yt[t % 2][:], AF.Square,
                                 accum_out=st[:, 2:3]).then_inc(ssd, 1)
                    s.wait_ge(varr, t + 1)
                    ser(s.activation(st[:, 7:8], st[:, 5:6], AF.Ln, bias=epst[:, 0:1]))
                    s.activation(st[:, 6:7], st[:, 7:8], AF.Exp, scale=-0.5).then_inc(rstdd, 1)

            @block.vector
            def _(v):
                ser = serializer(v, vser)
                for t in range(nt):
                    y = yt[t % 2]
                    v.wait_ge(yload, 16 * (t + 1))
                    ser(v.reduce_sum(st[:, 0:1], y[:], axis=AX.X))
                    ser(v.tensor_scalar(st[:, 1:2], st[:, 0:1], -1.0 / D, None, OP.mult))
                    v.wait_ge(ssd, t + 1)
                    v.tensor_scalar(st[:, 3:4], st[:, 2:3], 1.0 / D, None, OP.mult)
                    ser(v.tensor_tensor(st[:, 4:5], st[:, 1:2], st[:, 1:2], OP.mult))
                    v.tensor_tensor(st[:, 5:6], st[:, 3:4], st[:, 4:5],
                                    OP.subtract).then_inc(varr, 1)
                    v.wait_ge(rstdd, t + 1)
                    if t >= 2:
                        v.wait_ge(outd, 16 * (t - 1))
                    ser(v.tensor_scalar(t1[t % 2][:], y[:], st[:, 1:2], st[:, 6:7],
                                        OP.add, OP.mult))
                    v.wait_ge(xload, 16 * (t + 1))
                    v.tensor_tensor(t1[t % 2][:], t1[t % 2][:], xt[t % 2][:],
                                    OP.add).then_inc(resd, 1)

            @block.gpsimd
            def _(g):
                for t in range(nt):
                    g.wait_ge(resd, t + 1)
                    g.dma_start(out=out_dram_tiles[t], in_=t1[t % 2][:]).then_inc(outd, 16)
                g.wait_ge(outd, 16 * nt)


def make_sem_factory(nc, ctx, pool_size=None):
    state = {"i": 0}
    def sems(n):
        out = []
        for _ in range(n):
            out.append(ctx.enter_context(nc.semaphore(f"s{state['i']}_{id(ctx) % 997}")))
            state["i"] += 1
        return out
    return sems


def build_A(layer):
    nc = bass.Bass("TRN2", target_bir_lowering=False)
    CH = 512
    NCH = 24
    x = nc.dram_tensor("x", [NT, 128, D], F32, kind="ExternalInput")
    wt = nc.dram_tensor("wt", [NCH, 32, 128, CH], F32, kind="ExternalInput")
    ident_d = nc.dram_tensor("ident", [128, 128], BF16, kind="ExternalInput")
    lnw = {}
    if layer == 0:
        for nm in ("qw", "qb", "kw", "kb"):
            lnw[nm] = nc.dram_tensor(nm, [128, D], F32, kind="ExternalInput")
    qT_o = nc.dram_tensor("qT_o", [32, 128, NT * 128], BF16, kind="ExternalOutput")
    kT_o = nc.dram_tensor("kT_o", [32, 128, NT * 128], BF16, kind="ExternalOutput")
    v_o = nc.dram_tensor("v_o", [NT, 128, D], BF16, kind="ExternalOutput")
    qn2_o = nc.dram_tensor("qn2_o", [128, NT * H], F32, kind="ExternalOutput")
    km2_o = nc.dram_tensor("km2_o", [1, H], F32, kind="ExternalOutput")
    q_nat = nc.dram_tensor("q_nat", [NT, 128, D], BF16)
    k_nat = nc.dram_tensor("k_nat", [NT, 128, D], BF16)

    with contextlib.ExitStack() as ctx:
        sems = make_sem_factory(nc, ctx)
        ident = ctx.enter_context(nc.sbuf_tensor("identt", [128, 128], BF16))
        epst = ctx.enter_context(nc.sbuf_tensor("epst", [128, 1], F32))
        (ld0,) = sems(1)
        with nc.Block() as block:
            @block.sync
            def _(s):
                s.dma_start(out=ident[:], in_=ident_d.ap()).then_inc(ld0, 16)
                s.wait_ge(ld0, 16)

            @block.gpsimd
            def _(g):
                g.memset(epst[:], LN_EPS)

        with contextlib.ExitStack() as ctx2:
            lnT = ctx2.enter_context(nc.sbuf_tensor("lnT", [128, 32, NT * 128], BF16))
            emit_ln_lnT(nc, sems, "ln", [x.ap()[t] for t in range(NT)], lnT, ident, epst)

            def dst_of(ch, t):
                proj = ch // 8
                fc = (ch % 8) * CH
                dstn = (q_nat, k_nat, v_o)[proj]
                return dstn.ap()[t, :, fc:fc + CH]

            emit_proj_phase(nc, sems, "pj", lnT, wt, NCH, CH, dst_of, clip=(layer == 0))

        if layer == 0:
            wq = ctx.enter_context(nc.sbuf_tensor("wqv", [128, D], F32))
            bq = ctx.enter_context(nc.sbuf_tensor("bqv", [128, D], F32))
            wk = ctx.enter_context(nc.sbuf_tensor("wkv", [128, D], F32))
            bk = ctx.enter_context(nc.sbuf_tensor("bkv", [128, D], F32))
            (ld1,) = sems(1)
            with nc.Block() as block:
                @block.sync
                def _(s):
                    s.dma_start(out=wq[:], in_=lnw["qw"].ap()).then_inc(ld1, 16)
                    s.dma_start(out=bq[:], in_=lnw["qb"].ap()).then_inc(ld1, 16)
                    s.dma_start(out=wk[:], in_=lnw["kw"].ap()).then_inc(ld1, 16)
                    s.dma_start(out=bk[:], in_=lnw["kb"].ap()).then_inc(ld1, 16)
                    s.wait_ge(ld1, 64)
            emit_p2b_pass(nc, sems, "p2q", q_nat, qT_o, ident, True, wq[:], bq[:],
                          "q", qn2_o=qn2_o, epst=epst)
            emit_p2b_pass(nc, sems, "p2k", k_nat, kT_o, ident, True, wk[:], bk[:],
                          "k", km2_o=km2_o, epst=epst)
        else:
            emit_p2b_pass(nc, sems, "p2q", q_nat, qT_o, ident, False, None, None,
                          "q", qn2_o=qn2_o, epst=epst)
            emit_p2b_pass(nc, sems, "p2k", k_nat, kT_o, ident, False, None, None,
                          "k", km2_o=km2_o, epst=epst)
    return nc


def build_B(layer):
    nc = bass.Bass("TRN2", target_bir_lowering=False)
    CH = 256
    NCH = 16
    SBK = 4096
    x = nc.dram_tensor("x", [NT, 128, D], F32, kind="ExternalInput")
    qT = nc.dram_tensor("qT", [32, 128, NT * 128], BF16, kind="ExternalInput")
    KT = nc.dram_tensor("KT", [H, 128, SBK], BF16, kind="ExternalInput")
    Vf = nc.dram_tensor("Vf", [H, 128, 32 * 128], BF16, kind="ExternalInput")
    qn2 = nc.dram_tensor("qn2", [128, NT * H], F32, kind="ExternalInput")
    km2g = nc.dram_tensor("km2g", [4, H], F32, kind="ExternalInput")
    wo = nc.dram_tensor("wo", [NCH, 32, 128, CH], F32, kind="ExternalInput")
    ident_d = nc.dram_tensor("ident", [128, 128], BF16, kind="ExternalInput")
    slopes_d = iota_d = mask_d = qpos_d = None
    if layer == 0:
        iota_d = nc.dram_tensor("iota", [128, SBK], F32, kind="ExternalInput")
        mask_d = nc.dram_tensor("maskneg", [128, 512], F32, kind="ExternalInput")
        qpos_d = nc.dram_tensor("qpos", [128, NT], F32, kind="ExternalInput")
        slopes_d = nc.dram_tensor("slopes", [128, H], F32, kind="ExternalInput")
    x_new = nc.dram_tensor("x_new", [NT, 128, D], F32, kind="ExternalOutput")
    y_dram = nc.dram_tensor("y_dram", [NT, 128, D], BF16)

    nk_of = (lambda t: t + 1) if layer == 0 else (lambda t: 8)

    sc_at, tp_at, nrm_at, av_at, at_at = {}, {}, {}, {}, {}
    nsc = ntp = nht = 0
    for h in range(H):
        for t in range(NT):
            for j in range(nk_of(t)):
                nsc += 1
                sc_at[(h, t, j)] = nsc
            for g in range(nk_of(t)):
                ntp += 1
                tp_at[(h, t, g)] = ntp
            nht += 1
            nrm_at[(h, t)] = nht
            av_at[(h, t)] = nht
            at_at[(h, t)] = nht

    with contextlib.ExitStack() as ctx:
        sems = make_sem_factory(nc, ctx)
        ident = ctx.enter_context(nc.sbuf_tensor("identt", [128, 128], BF16))
        attnT = ctx.enter_context(nc.sbuf_tensor("attnT", [128, 32, NT * 128], BF16))
        nsh = ctx.enter_context(nc.sbuf_tensor("nsh", [128, NT, H], F32))
        epst = ctx.enter_context(nc.sbuf_tensor("epst", [128, 1], F32))

        with contextlib.ExitStack() as c1:
            qnb = c1.enter_context(nc.sbuf_tensor("qnb", [128, NT * H], F32))
            km4 = c1.enter_context(nc.sbuf_tensor("km4", [4, H], F32))
            km1 = c1.enter_context(nc.sbuf_tensor("km1", [1, H], F32))
            kmb = c1.enter_context(nc.sbuf_tensor("kmb", [128, H], F32))
            ones1 = c1.enter_context(nc.sbuf_tensor("ones1", [1, 128], F32))
            slp = c1.enter_context(nc.sbuf_tensor("slp", [128, H], F32))
            qps = c1.enter_context(nc.sbuf_tensor("qps", [128, NT], F32))
            sq = c1.enter_context(nc.sbuf_tensor("sqt", [128, H], F32))
            ps_bc = c1.enter_context(nc.psum_tensor("ps_bc", [128, H], F32))
            ld0, shd, sser1, vser1 = sems(4)

            nld = 16 * (5 if layer == 0 else 3)
            with nc.Block() as block:
                @block.sync
                def _(s):
                    s.dma_start(out=ident[:], in_=ident_d.ap()).then_inc(ld0, 16)
                    s.dma_start(out=qnb[:], in_=qn2.ap()).then_inc(ld0, 16)
                    s.dma_start(out=km4[:], in_=km2g.ap()).then_inc(ld0, 16)
                    if layer == 0:
                        s.dma_start(out=qps[:], in_=qpos_d.ap()).then_inc(ld0, 16)
                        s.dma_start(out=slp[:], in_=slopes_d.ap()).then_inc(ld0, 16)

                @block.gpsimd
                def _(g):
                    g.memset(ones1[:], 1.0)
                    g.memset(epst[:], LN_EPS)
                    g.wait_ge(ld0, nld)
                    g.tensor_reduce(km1[:], km4[:], axis=AX.C, op=OP.max).then_inc(shd, 1)

                @block.scalar
                def _(s):
                    ser = serializer(s, sser1)
                    s.wait_ge(shd, 1)
                    ser(s.activation(km1[:], km1[:], AF.Sqrt))
                    s.mul(km1[:], km1[:], -SCALE * 1.03).then_inc(shd, 1)
                    s.activation(qnb[:], qnb[:], AF.Sqrt).then_inc(shd, 1)

                @block.tensor
                def _(te):
                    te.wait_ge(shd, 2)
                    te.matmul(ps_bc[:], ones1[:], km1[:], start=True, stop=True).then_inc(shd, 1)

                @block.vector
                def _(v):
                    ser = serializer(v, vser1)
                    v.wait_ge(shd, 4)
                    ser(v.tensor_copy(kmb[:], ps_bc[:]))
                    for t in range(NT):
                        ser(v.tensor_tensor(nsh[:, t, :], qnb[:, t * H:(t + 1) * H], kmb[:],
                                            OP.mult))
                        if layer == 0:
                            ser(v.tensor_scalar(sq[:], slp[:], qps[:, t:t + 1], None, OP.mult))
                            ser(v.tensor_tensor(nsh[:, t, :], nsh[:, t, :], sq[:], OP.subtract))

        with contextlib.ExitStack() as c2:
            qTh = [c2.enter_context(nc.sbuf_tensor(f"qTh{i}", [128, NT * 128], BF16)) for i in range(2)]
            KTh = [c2.enter_context(nc.sbuf_tensor(f"KTh{i}", [128, SBK], BF16)) for i in range(2)]
            Vh = [c2.enter_context(nc.sbuf_tensor(f"Vh{i}", [128, 32, 128], BF16)) for i in range(2)]
            P = c2.enter_context(nc.sbuf_tensor("P", [128, SBK], BF16))
            PT = c2.enter_context(nc.sbuf_tensor("PT", [128, 32, 128], BF16))
            tf = [c2.enter_context(nc.sbuf_tensor(f"tf{i}", [128, 512], F32)) for i in range(2)]
            rs = c2.enter_context(nc.sbuf_tensor("rs", [128, 12], F32))
            alib = None
            iot = msk = None
            if layer == 0:
                alib = c2.enter_context(nc.sbuf_tensor("alib", [128, SBK], F32))
                iot = c2.enter_context(nc.sbuf_tensor("iot", [128, SBK], F32))
                msk = c2.enter_context(nc.sbuf_tensor("msk", [128, 512], F32))
            ps_sc = [c2.enter_context(nc.psum_tensor(f"ps_sc{i}", [128, 512], F32)) for i in range(2)]
            ps_pt = [c2.enter_context(nc.psum_tensor(f"ps_pt{i}", [128, 512], BF16)) for i in range(2)]
            ps_av = c2.enter_context(nc.psum_tensor("ps_av", [128, 128], F32))
            hload, scd, alid, expd, nrmd, tpd, ptd, avd, atd, alibd, mld, vser2 = sems(12)

            with nc.Block() as block:
                @block.sync
                def _(s):
                    if layer == 0:
                        s.dma_start(out=iot[:], in_=iota_d.ap()).then_inc(mld, 16)
                        s.dma_start(out=msk[:], in_=mask_d.ap()).then_inc(mld, 16)
                    for h in range(H):
                        if h >= 2:
                            s.wait_ge(atd, at_at[(h - 2, NT - 1)])
                        s.dma_start(out=qTh[h % 2][:], in_=qT.ap()[h]).then_inc(hload, 16)
                        s.dma_start(out=KTh[h % 2][:], in_=KT.ap()[h]).then_inc(hload, 16)
                        s.dma_start(out=Vh[h % 2][:].rearrange("p a b -> p (a b)"),
                                    in_=Vf.ap()[h]).then_inc(hload, 16)

                @block.vector
                def _(v):
                    ser = serializer(v, vser2)
                    for h in range(H):
                        if layer == 0:
                            if h == 0:
                                v.wait_ge(mld, 32)
                            ser(v.tensor_scalar(alib[:], iot[:], float(SLOPES[h] / SCALE),
                                                None, OP.mult))
                            v.sem_inc(alibd, 1)
                        for t in range(NT):
                            nk = nk_of(t)
                            if layer == 0:
                                for j in range(nk):
                                    i = sc_at[(h, t, j)]
                                    v.wait_ge(scd, i)
                                    if i >= 2:
                                        v.wait_ge(expd, i - 1)
                                    if j == t:
                                        ser(v.tensor_tensor(tf[i % 2][:], ps_sc[i % 2][:],
                                                            alib[:, 512 * j:512 * (j + 1)], OP.add))
                                        v.tensor_tensor(tf[i % 2][:], tf[i % 2][:], msk[:],
                                                        OP.add).then_inc(alid, 1)
                                    else:
                                        v.tensor_tensor(tf[i % 2][:], ps_sc[i % 2][:],
                                                        alib[:, 512 * j:512 * (j + 1)],
                                                        OP.add).then_inc(alid, 1)
                            v.wait_ge(expd, sc_at[(h, t, nk - 1)])
                            ser(v.reduce_sum(rs[:, 8:9], rs[:, 0:nk], axis=AX.X))
                            ser(v.reciprocal(rs[:, 9:10], rs[:, 8:9]))
                            v.tensor_scalar(P[:, :512 * nk], P[:, :512 * nk], rs[:, 9:10],
                                            None, OP.mult).then_inc(nrmd, 1)
                            for g in range(nk):
                                gi = tp_at[(h, t, g)]
                                v.wait_ge(tpd, gi)
                                v.tensor_copy(PT[:, 4 * g:4 * (g + 1), :],
                                              ps_pt[gi % 2][:].rearrange("p (a b) -> p a b", a=4)
                                              ).then_inc(ptd, 1)
                            v.wait_ge(avd, av_at[(h, t)])
                            v.tensor_copy(attnT[:, h, 128 * t:128 * (t + 1)],
                                          ps_av[:]).then_inc(atd, 1)

                @block.scalar
                def _(s):
                    for h in range(H):
                        for t in range(NT):
                            nk = nk_of(t)
                            for j in range(nk):
                                i = sc_at[(h, t, j)]
                                if (h, t) != (0, 0) and j == 0:
                                    prev = (h, t - 1) if t > 0 else (h - 1, NT - 1)
                                    s.wait_ge(tpd, tp_at[(prev[0], prev[1], nk_of(prev[1]) - 1)])
                                if layer == 0:
                                    s.wait_ge(alid, i)
                                    src_ap = tf[i % 2][:]
                                else:
                                    s.wait_ge(scd, i)
                                    src_ap = ps_sc[i % 2][:]
                                s.activation(P[:, 512 * j:512 * (j + 1)], src_ap, AF.Exp,
                                             bias=nsh[:, t, h:h + 1], scale=SCALE,
                                             accum_out=rs[:, j:j + 1]).then_inc(expd, 1)

                @block.tensor
                def _(te):
                    for h in range(H):
                        te.wait_ge(hload, 48 * (h + 1))
                        if layer == 0:
                            te.wait_ge(alibd, h + 1)
                        for t in range(NT):
                            nk = nk_of(t)
                            for j in range(nk):
                                i = sc_at[(h, t, j)]
                                if i >= 3:
                                    te.wait_ge(alid if layer == 0 else expd, i - 2)
                                te.matmul(ps_sc[i % 2][:],
                                          qTh[h % 2][:, 128 * t:128 * (t + 1)],
                                          KTh[h % 2][:, 512 * j:512 * (j + 1)],
                                          start=True, stop=True).then_inc(scd, 1)
                            te.wait_ge(nrmd, nrm_at[(h, t)])
                            for g in range(nk):
                                gi = tp_at[(h, t, g)]
                                if gi >= 3:
                                    te.wait_ge(ptd, gi - 2)
                                inst = None
                                for jj in range(4):
                                    kt = 4 * g + jj
                                    inst = te.transpose(ps_pt[gi % 2][:, 128 * jj:128 * (jj + 1)],
                                                        P[:, 128 * kt:128 * (kt + 1)], ident[:])
                                inst.then_inc(tpd, 1)
                            if at_at[(h, t)] >= 2:
                                te.wait_ge(atd, at_at[(h, t)] - 1)
                            for kt in range(4 * nk):
                                if kt % 4 == 0:
                                    te.wait_ge(ptd, tp_at[(h, t, kt // 4)])
                                inst = te.matmul(ps_av[:], Vh[h % 2][:, kt, :], PT[:, kt, :],
                                                 start=(kt == 0), stop=(kt == 4 * nk - 1))
                            inst.then_inc(avd, 1)

        def dst_of(ch, t):
            return y_dram.ap()[t, :, ch * CH:(ch + 1) * CH]

        emit_proj_phase(nc, sems, "op", attnT, wo, NCH, CH, dst_of, clip=False)

        emit_ln_residual(nc, sems, "lr", [y_dram.ap()[t] for t in range(NT)],
                         [x.ap()[t] for t in range(NT)],
                         [x_new.ap()[t] for t in range(NT)], epst=epst)
    return nc


def build_C():
    nc = bass.Bass("TRN2", target_bir_lowering=False)
    CH = 256
    NCH = 16
    x = nc.dram_tensor("x", [NT, 128, D], F32, kind="ExternalInput")
    vals = nc.dram_tensor("vals", [NCH, 32, 128, CH], I32, kind="ExternalInput")
    sc4 = nc.dram_tensor("sc4", [128, D], F32, kind="ExternalInput")
    sel4 = nc.dram_tensor("sel4", [128, 32 * 128], BF16, kind="ExternalInput")
    bias_b = nc.dram_tensor("bias_b", [128, D], F32, kind="ExternalInput")
    ident_d = nc.dram_tensor("ident", [128, 128], BF16, kind="ExternalInput")
    x_new = nc.dram_tensor("x_new", [NT, 128, D], F32, kind="ExternalOutput")
    y_dram = nc.dram_tensor("y_dram", [NT, 128, D], BF16)

    with contextlib.ExitStack() as ctx:
        sems = make_sem_factory(nc, ctx)
        ident = ctx.enter_context(nc.sbuf_tensor("identt", [128, 128], BF16))
        biasb = ctx.enter_context(nc.sbuf_tensor("biasb", [128, D], F32))
        sc4b = ctx.enter_context(nc.sbuf_tensor("sc4b", [128, D], BF16))
        sel4b = ctx.enter_context(nc.sbuf_tensor("sel4b", [128, 32, 128], BF16))
        epst = ctx.enter_context(nc.sbuf_tensor("epst", [128, 1], F32))
        epst2 = ctx.enter_context(nc.sbuf_tensor("epst2", [128, 1], F32))
        (ld0,) = sems(1)
        with nc.Block() as block:
            @block.gpsimd
            def _(g):
                g.memset(epst[:], LN_EPS)

            @block.sync
            def _(s):
                s.dma_start(out=ident[:], in_=ident_d.ap()).then_inc(ld0, 16)
                s.dma_start(out=biasb[:], in_=bias_b.ap()).then_inc(ld0, 16)
                s.wait_ge(ld0, 32)

            @block.gpsimd
            def _(g):
                g.memset(epst2[:], 0.0)
                g.dma_start(out=sc4b[:], in_=sc4.ap()).then_inc(ld0, 16)
                g.dma_start(out=sel4b[:].rearrange("p a b -> p (a b)"),
                            in_=sel4.ap()).then_inc(ld0, 16)
                g.wait_ge(ld0, 64)

        with contextlib.ExitStack() as c2:
            lnT = c2.enter_context(nc.sbuf_tensor("lnT", [128, 32, NT * 128], BF16))
            emit_ln_lnT(nc, sems, "ln", [x.ap()[t] for t in range(NT)], lnT, ident, epst)

            vf = [c2.enter_context(nc.sbuf_tensor(f"vf{i}", [128, 32, CH], BF16)) for i in range(2)]
            wde = [c2.enter_context(nc.sbuf_tensor(f"wde{i}", [128, 32, CH], BF16)) for i in range(2)]
            sctl = [c2.enter_context(nc.sbuf_tensor(f"sctl{i}", [128, 32, CH], BF16)) for i in range(2)]
            stage = [c2.enter_context(nc.sbuf_tensor(f"stg{i}", [128, CH], BF16)) for i in range(2)]
            ps = [c2.enter_context(nc.psum_tensor(f"cps{i}", [128, CH], F32)) for i in range(2)]
            ps_s = [c2.enter_context(nc.psum_tensor(f"csps{i}", [128, CH], F32)) for i in range(2)]
            vload, scmm, sevd, deqd, mmd, evd, outd, vser3 = sems(8)

            with nc.Block() as block:
                @block.gpsimd
                def _(g):
                    for ch in range(NCH):
                        if ch >= 2:
                            g.wait_ge(deqd, ch - 1)
                        g.dma_start(out=vf[ch % 2][:, :16],
                                    in_=vals.ap()[ch, :16].rearrange("a p c -> p a c")
                                    ).then_inc(vload, 16)
                        g.dma_start(out=vf[ch % 2][:, 16:],
                                    in_=vals.ap()[ch, 16:].rearrange("a p c -> p a c")
                                    ).then_inc(vload, 16)

                @block.tensor
                def _(te):
                    for ch in range(NCH):
                        for d in range(32):
                            sidx = ch * 32 + d
                            if sidx >= 2:
                                te.wait_ge(sevd, sidx - 1)
                            te.matmul(ps_s[sidx % 2][:], sel4b[:, d, :],
                                      sc4b[:, ch * CH:(ch + 1) * CH],
                                      start=True, stop=True).then_inc(scmm, 1)
                        te.wait_ge(deqd, ch + 1)
                        for t in range(NT):
                            idx = ch * NT + t
                            if idx >= 2:
                                te.wait_ge(evd, idx - 1)
                            pb = ps[idx % 2]
                            for d in range(32):
                                inst = te.matmul(pb[:], lnT[:, d, 128 * t:128 * (t + 1)],
                                                 wde[ch % 2][:, d, :],
                                                 start=(d == 0), stop=(d == 31))
                            inst.then_inc(mmd, 1)

                @block.vector
                def _(v):
                    ser = serializer(v, vser3)
                    for ch in range(NCH):
                        for d in range(32):
                            sidx = ch * 32 + d
                            v.wait_ge(scmm, sidx + 1)
                            inst = v.tensor_copy(sctl[ch % 2][:, d, :],
                                                 ps_s[sidx % 2][:])
                            if d == 31:
                                ser(inst)
                                v.sem_inc(sevd, 1)
                            else:
                                inst.then_inc(sevd, 1)
                        v.wait_ge(vload, 32 * (ch + 1))
                        if ch >= 2:
                            v.wait_ge(mmd, NT * (ch - 1))
                        ser(v.tensor_scalar(vf[ch % 2][:].rearrange("p a b -> p (a b)"),
                                            vf[ch % 2][:].rearrange("p a b -> p (a b)"),
                                            -128.0, None, OP.add))
                        v.tensor_tensor(wde[ch % 2][:].rearrange("p a b -> p (a b)"),
                                        vf[ch % 2][:].rearrange("p a b -> p (a b)"),
                                        sctl[ch % 2][:].rearrange("p a b -> p (a b)"),
                                        OP.mult).then_inc(deqd, 1)
                        for t in range(NT):
                            idx = ch * NT + t
                            v.wait_ge(mmd, idx + 1)
                            if idx >= 2:
                                v.wait_ge(outd, 16 * (idx - 1))
                            v.tensor_tensor(stage[idx % 2][:], ps[idx % 2][:],
                                            biasb[:, ch * CH:(ch + 1) * CH],
                                            OP.add).then_inc(evd, 1)

                @block.sync
                def _(s):
                    for ch in range(NCH):
                        for t in range(NT):
                            idx = ch * NT + t
                            s.wait_ge(evd, idx + 1)
                            s.dma_start(out=y_dram.ap()[t, :, ch * CH:(ch + 1) * CH],
                                        in_=stage[idx % 2][:]).then_inc(outd, 16)
                    s.wait_ge(outd, 16 * NCH * NT)

        emit_ln_residual(nc, sems, "lr", [y_dram.ap()[t] for t in range(NT)],
                         [x.ap()[t] for t in range(NT)],
                         [x_new.ap()[t] for t in range(NT)], epst=epst)
    return nc


class Chain:
    def __init__(self):
        self.items = []
    def add(self, eng, fn):
        self.items.append((eng, fn))
    def emit(self, nc, seq):
        items = self.items
        with nc.Block() as block:
            for engname in ("tensor", "vector", "scalar", "gpsimd", "sync"):
                ops = [(i, fn) for i, (e, fn) in enumerate(items) if e == engname]
                if not ops:
                    continue
                def mk(ops):
                    def _(eng):
                        for i, fn in ops:
                            if i > 0:
                                eng.wait_ge(seq, i)
                            inst = fn(eng)
                            inst.then_inc(seq, 1)
                    return _
                getattr(block, engname)(mk(ops))


def build_D():
    nc = bass.Bass("TRN2", target_bir_lowering=False)
    CH = 256
    NCH = 16
    x = nc.dram_tensor("x", [NT, 128, D], F32, kind="ExternalInput")
    wb_t = nc.dram_tensor("wb_t", [NCH, 32, 128, CH], F32, kind="ExternalInput")
    wgt = nc.dram_tensor("wgt", [128, 32, 16], F32, kind="ExternalInput")
    at_t = nc.dram_tensor("at_t", [128, 32, 128], F32, kind="ExternalInput")
    bf_t = nc.dram_tensor("bf_t", [128, D], F32, kind="ExternalInput")
    emat = nc.dram_tensor("emat", [8, 128], BF16, kind="ExternalInput")
    bias_b = nc.dram_tensor("bias_b", [128, D], F32, kind="ExternalInput")
    bthr = nc.dram_tensor("bthr", [128, 1], F32, kind="ExternalInput")
    ident_d = nc.dram_tensor("ident", [128, 128], BF16, kind="ExternalInput")
    out_o = nc.dram_tensor("out_o", [NT, 128, D], F32, kind="ExternalOutput")
    y_dram = nc.dram_tensor("y_dram", [NT, 128, D], BF16)

    with contextlib.ExitStack() as ctx:
        sems = make_sem_factory(nc, ctx)
        ident = ctx.enter_context(nc.sbuf_tensor("identt", [128, 128], BF16))
        biasb = ctx.enter_context(nc.sbuf_tensor("biasb", [128, D], F32))
        gT = ctx.enter_context(nc.sbuf_tensor("gT", [128, NT, 128], BF16))
        bfb = ctx.enter_context(nc.sbuf_tensor("bfb", [128, D], BF16))
        epst = ctx.enter_context(nc.sbuf_tensor("epst", [128, 1], F32))
        (ld0,) = sems(1)

        with contextlib.ExitStack() as c2:
            wgtb = c2.enter_context(nc.sbuf_tensor("wgtb", [128, 32, 16], BF16))
            atb = c2.enter_context(nc.sbuf_tensor("atb", [128, 32, 128], BF16))
            ematb = c2.enter_context(nc.sbuf_tensor("ematb", [8, 128], BF16))
            bthrb = c2.enter_context(nc.sbuf_tensor("bthrb", [128, 1], F32))

            with nc.Block() as block:
                @block.sync
                def _(s):
                    s.dma_start(out=ident[:], in_=ident_d.ap()).then_inc(ld0, 16)
                    s.dma_start(out=biasb[:], in_=bias_b.ap()).then_inc(ld0, 16)
                    s.dma_start(out=ematb[:], in_=emat.ap()).then_inc(ld0, 16)
                    s.dma_start(out=bthrb[:], in_=bthr.ap()).then_inc(ld0, 16)

                @block.gpsimd
                def _(g):
                    g.memset(epst[:], LN_EPS)
                    g.dma_start(out=wgtb[:].rearrange("p a b -> p (a b)"),
                                in_=wgt.ap().rearrange("p a b -> p (a b)")).then_inc(ld0, 16)
                    g.dma_start(out=atb[:].rearrange("p a b -> p (a b)"),
                                in_=at_t.ap().rearrange("p a b -> p (a b)")).then_inc(ld0, 16)
                    g.dma_start(out=bfb[:], in_=bf_t.ap()).then_inc(ld0, 16)
                    g.wait_ge(ld0, 112)

            lnT = c2.enter_context(nc.sbuf_tensor("lnT", [128, 32, NT * 128], BF16))
            emit_ln_lnT(nc, sems, "ln", [x.ap()[t] for t in range(NT)], lnT, ident, epst)

            gth = c2.enter_context(nc.sbuf_tensor("gth", [128, 16], F32))
            ge8 = c2.enter_context(nc.sbuf_tensor("ge8", [128, 8], F32))
            sst = c2.enter_context(nc.sbuf_tensor("sst", [128, 8], F32))
            hb = c2.enter_context(nc.sbuf_tensor("hb", [128, 128], BF16))
            hTb = c2.enter_context(nc.sbuf_tensor("hTb", [128, 128], BF16))
            w8 = c2.enter_context(nc.sbuf_tensor("w8", [128, 8], BF16))
            w8T = c2.enter_context(nc.sbuf_tensor("w8T", [8, 128], BF16))
            weT = c2.enter_context(nc.sbuf_tensor("weT", [128, 128], BF16))
            ps_g = c2.enter_context(nc.psum_tensor("ps_g", [128, 16], F32))
            ps_h = c2.enter_context(nc.psum_tensor("ps_h", [128, 128], F32))
            ps_t1 = c2.enter_context(nc.psum_tensor("ps_t1", [8, 128], BF16))
            ps_t2 = c2.enter_context(nc.psum_tensor("ps_t2", [128, 128], F32))
            ps_t3 = c2.enter_context(nc.psum_tensor("ps_t3", [128, 128], BF16))
            (seq,) = sems(1)

            chain = Chain()
            for t in range(NT):
                def mms(te, t=t):
                    for d in range(32):
                        te.matmul(ps_g[:, :9], lnT[:, d, 128 * t:128 * (t + 1)],
                                  wgtb[:, d, :9], start=(d == 0), stop=(d == 31))
                    for d in range(32):
                        inst = te.matmul(ps_h[:], lnT[:, d, 128 * t:128 * (t + 1)],
                                         atb[:, d, :], start=(d == 0), stop=(d == 31))
                    return inst
                chain.add("tensor", mms)
                chain.add("vector", lambda v: v.tensor_copy(gth[:, :9], ps_g[:, :9]))
                chain.add("vector", lambda v: v.reduce_max(sst[:, 0:1], gth[:, 0:8], axis=AX.X))
                chain.add("vector", lambda v: v.tensor_scalar(sst[:, 1:2], sst[:, 0:1],
                                                              -1.0, None, OP.mult))
                chain.add("scalar", lambda s: s.activation(ge8[:], gth[:, 0:8], AF.Exp,
                                                           bias=sst[:, 1:2],
                                                           accum_out=sst[:, 2:3]))
                chain.add("scalar", lambda s: s.activation(sst[:, 3:4], gth[:, 8:9],
                                                           AF.Sigmoid, bias=bthrb[:]))
                chain.add("vector", lambda v: v.reciprocal(sst[:, 4:5], sst[:, 2:3]))
                chain.add("vector", lambda v: v.tensor_scalar(ge8[:], ge8[:], sst[:, 4:5], None, OP.mult))
                chain.add("vector", lambda v: v.tensor_scalar(sst[:, 5:6], sst[:, 3:4], 0.125, None, OP.mult))
                chain.add("vector", lambda v: v.tensor_scalar(ge8[:], ge8[:], sst[:, 5:6], None, OP.subtract))
                chain.add("vector", lambda v: v.tensor_scalar(ge8[:], ge8[:], 0.0, None, OP.max))
                chain.add("vector", lambda v: v.reduce_sum(sst[:, 6:7], ge8[:], axis=AX.X))
                chain.add("vector", lambda v: v.tensor_scalar(sst[:, 6:7], sst[:, 6:7], 1e-30, None, OP.max))
                chain.add("vector", lambda v: v.reciprocal(sst[:, 7:8], sst[:, 6:7]))
                chain.add("vector", lambda v: v.tensor_scalar(w8[:], ge8[:], sst[:, 7:8], 2.0, OP.mult, OP.mult))
                chain.add("tensor", lambda te: te.transpose(ps_t1[:], w8[:], ident[:]))
                chain.add("vector", lambda v: v.tensor_copy(w8T[:], ps_t1[:]))
                chain.add("tensor", lambda te: te.matmul(ps_t2[:], ematb[:], w8T[:],
                                                         start=True, stop=True))
                chain.add("vector", lambda v: v.tensor_copy(weT[:], ps_t2[:]))
                chain.add("vector", lambda v: v.tensor_copy(hb[:], ps_h[:]))
                chain.add("tensor", lambda te: te.transpose(ps_t3[:], hb[:], ident[:]))
                chain.add("vector", lambda v: v.tensor_copy(hTb[:], ps_t3[:]))
                chain.add("vector",
                          lambda v, t=t: v.tensor_tensor(gT[:, t, :], hTb[:], weT[:], OP.mult))
            chain.emit(nc, seq)

            def extra_mm(te, ch, t, pb):
                return te.matmul(pb[:], gT[:, t, :], bfb[:, ch * CH:(ch + 1) * CH],
                                 start=False, stop=True)

            def evac_extra(v, ch, t, stg):
                return v.tensor_tensor(stg[:], stg[:], biasb[:, ch * CH:(ch + 1) * CH],
                                       OP.add)

            def dst_of(ch, t):
                return y_dram.ap()[t, :, ch * CH:(ch + 1) * CH]

            emit_proj_phase(nc, sems, "bp", lnT, wb_t, NCH, CH, dst_of, clip=False,
                            extra_mm=extra_mm, evac_extra=evac_extra)

        emit_ln_residual(nc, sems, "lr", [y_dram.ap()[t] for t in range(NT)],
                         [x.ap()[t] for t in range(NT)],
                         [out_o.ap()[t] for t in range(NT)], epst=epst)
    return nc


"""Single-launch fused megakernel: pair-island design.

Core c: pair {2i,2i+1}; batch c//4. Pair covers its whole batch (4096 tok).
Core's L0-set = 16 tiles, POSITION ORDER [4k+lane0 (k<8), 4k+lane1 (k<8)],
lane0 = c%4 (own output lane), lane1 = lane0 XOR 2. Output tiles = pos 0..7.
Cross-core exchange only within the pair via addr_space="Shared" DRAM +
polling barriers (no chip-wide collectives available in this runtime).
"""
import contextlib
import numpy as np
import ml_dtypes
import concourse.bass as bass
import concourse.mybir as mybir


NTL = 16
NTO = 8
MAGIC = 0x0C0FFEE1


def emit_barrier(nc, sems, tag, flags_sh, scratch_i32, myoff_sb, nonce_col):
    (bsem,) = sems(1)
    with nc.Block() as block:
        @block.gpsimd
        def _(g):
            with g.register(f"{tag}off") as off, g.register(f"{tag}ra") as ra, \
                 g.register(f"{tag}rb") as rb, g.register(f"{tag}cond") as cond, \
                 g.register(f"{tag}tgt") as tgt, g.register(f"{tag}nnc") as nnc:
                g.reg_load(off, myoff_sb[:, 0:1])
                g.reg_load(nnc, myoff_sb[:, nonce_col:nonce_col + 1])
                g.dma_start(out=bass.AP(flags_sh, off, [[1, 1]]),
                            in_=myoff_sb[:, nonce_col:nonce_col + 1]).then_inc(bsem, 16)
                g.wait_ge(bsem, 16)
                g.reg_mov(cond, 1)
                g.reg_mov(tgt, 16)
                with g.While(cond):
                    g.reg_add(tgt, tgt, 16)
                    g.dma_start(out=scratch_i32[:, 16:48],
                                in_=flags_sh.ap().rearrange("a b -> (a b)").unsqueeze(0)
                                ).then_inc(bsem, 16)
                    g.wait_ge(bsem, tgt)
                    g.reg_load(ra, scratch_i32[:, 16:17])
                    g.reg_load(rb, scratch_i32[:, 32:33])
                    g.reg_alu(ra, ra, nnc, OP.bitwise_xor)
                    g.reg_alu(rb, rb, nnc, OP.bitwise_xor)
                    g.reg_alu(cond, ra, rb, OP.bitwise_or)


def emit_vshuffle(nc, sems, tag, v_nat, v_stg, v_sh, myoff_sb, nt):
    vsem, vcp = sems(2)
    with nc.Block() as block:
        @block.sync
        def _(s):
            for ti in range(nt):
                s.dma_start(
                    out=v_stg.ap().rearrange("a p b c -> p a b c")[:, :, ti, :],
                    in_=v_nat.ap()[ti].rearrange("p (h e) -> p h e", e=128)
                    ).then_inc(vsem, 16)
            s.wait_ge(vsem, 16 * nt)

        @block.gpsimd
        def _(g):
            g.wait_ge(vsem, 16 * nt)
            with g.register(f"{tag}pv") as pv:
                g.reg_load(pv, myoff_sb[:, 1:2])
                g.dma_start(
                    out=bass.AP(v_sh, pv, [[1, 8388608]]),
                    in_=v_stg.ap().rearrange("a p b c -> (a p b c)").unsqueeze(0)
                    ).then_inc(vcp, 16)
                g.wait_ge(vcp, 16)


def emit_shift_prep(nc, sems, tag, layer, nt, qn2_d, km_sh, nsh, consts, epst):
        with contextlib.ExitStack() as c1:
            qnb = c1.enter_context(nc.sbuf_tensor(f"{tag}qnb", [128, nt * H], F32))
            km4 = c1.enter_context(nc.sbuf_tensor(f"{tag}km4", [2, H], F32))
            km1 = c1.enter_context(nc.sbuf_tensor(f"{tag}km1", [1, H], F32))
            kmb = c1.enter_context(nc.sbuf_tensor(f"{tag}kmb", [128, H], F32))
    
            sq = c1.enter_context(nc.sbuf_tensor(f"{tag}sqt", [128, H], F32))
            ps_bc = c1.enter_context(nc.psum_tensor(f"{tag}ps_bc", [128, H], F32))
            ld0, shd, sser1, vser1 = sems(4)

            with nc.Block() as block:
                @block.sync
                def _(s):
                    s.dma_start(out=qnb[:], in_=qn2_d.ap()).then_inc(ld0, 16)
                    s.dma_start(out=km4[:], in_=km_sh.ap()).then_inc(ld0, 16)

                @block.gpsimd
                def _(g):
                    g.wait_ge(ld0, 32)
                    g.tensor_reduce(km1[:], km4[:], axis=AX.C, op=OP.max).then_inc(shd, 1)

                @block.scalar
                def _(s):
                    ser = serializer(s, sser1)
                    s.wait_ge(shd, 1)
                    ser(s.activation(km1[:], km1[:], AF.Sqrt))
                    s.mul(km1[:], km1[:], -SCALE * 1.03).then_inc(shd, 1)
                    s.activation(qnb[:], qnb[:], AF.Sqrt).then_inc(shd, 1)

                @block.tensor
                def _(te):
                    te.wait_ge(shd, 2)
                    te.matmul(ps_bc[:], consts['ones'][:], km1[:], start=True, stop=True).then_inc(shd, 1)

                @block.vector
                def _(v):
                    ser = serializer(v, vser1)
                    v.wait_ge(shd, 4)
                    ser(v.tensor_copy(kmb[:], ps_bc[:]))
                    for t in range(nt):
                        ser(v.tensor_tensor(nsh[:, t, :], qnb[:, t * H:(t + 1) * H],
                                            kmb[:], OP.mult))
                        if layer == 0:
                            ser(v.tensor_scalar(sq[:], consts["slp"][:],
                                                consts["qps"][:, t:t + 1], None, OP.mult))
                            ser(v.tensor_tensor(nsh[:, t, :], nsh[:, t, :], sq[:],
                                                OP.subtract))


def emit_attention(nc, sems, tag, layer, positions, nseg_of, qT_d, qT_tok0,
                   KT_d, V_d, nsh, attnT, ident, consts):
    npos = len(positions)
    L0 = (layer == 0)
    sc_at, tp_at, nrm_at, at_at = {}, {}, {}, {}
    nsc = ntp = nht = 0
    for h in range(H):
        for pi, p in enumerate(positions):
            nch = nseg_of(p)
            for seg in range(2):
                for c in range(nch):
                    nsc += 1
                    sc_at[(h, pi, seg, c)] = nsc
                for c in range(nch):
                    ntp += 1
                    tp_at[(h, pi, seg, c)] = ntp
            nht += 1
            nrm_at[(h, pi)] = nht
            at_at[(h, pi)] = nht

    with contextlib.ExitStack() as c2:
        qTh = [c2.enter_context(nc.sbuf_tensor(f"{tag}qTh{i}", [128, npos * 128], BF16)) for i in range(2)]
        KTh = [c2.enter_context(nc.sbuf_tensor(f"{tag}KTh{i}", [128, 4096], BF16)) for i in range(2)]
        Vh = [c2.enter_context(nc.sbuf_tensor(f"{tag}Vh{i}", [128, 32, 128], BF16)) for i in range(2)]
        P = c2.enter_context(nc.sbuf_tensor(f"{tag}P", [128, 4096], BF16))
        PT = c2.enter_context(nc.sbuf_tensor(f"{tag}PT", [128, 32, 128], BF16))
        tf = [c2.enter_context(nc.sbuf_tensor(f"{tag}tf{i}", [128, 512], F32)) for i in range(2)]
        rs = c2.enter_context(nc.sbuf_tensor(f"{tag}rs", [128, 12], F32))
        if L0:
            alib = c2.enter_context(nc.sbuf_tensor(f"{tag}alib", [128, 4096], F32))
        ps_sc = [c2.enter_context(nc.psum_tensor(f"{tag}ps_sc{i}", [128, 512], F32)) for i in range(2)]
        ps_pt = [c2.enter_context(nc.psum_tensor(f"{tag}ps_pt{i}", [128, 512], BF16)) for i in range(2)]
        ps_av = c2.enter_context(nc.psum_tensor(f"{tag}ps_av", [128, 128], F32))
        hload, scd, alid, expd, nrmd, tpd, ptd, avd, atd, alibd, vser2 = sems(11)

        def chunk_cols(seg, c):
            return (2048 * seg + 512 * c, 2048 * seg + 512 * (c + 1))

        with nc.Block() as block:
            @block.sync
            def _(s):
                for h in range(H):
                    if h >= 2:
                        s.wait_ge(atd, at_at[(h - 2, npos - 1)])
                    s.dma_start(out=qTh[h % 2][:],
                                in_=qT_d.ap()[h, :, qT_tok0:qT_tok0 + npos * 128]
                                ).then_inc(hload, 16)
                    for par in range(2):
                        s.dma_start(out=KTh[h % 2][:, 2048 * par:2048 * (par + 1)],
                                    in_=bass.AP(KT_d, par * 8388608 + h * 262144,
                                                [[2048, 128], [1, 2048]])
                                    ).then_inc(hload, 16)
                        s.dma_start(out=Vh[h % 2][:, 16 * par:16 * (par + 1), :]
                                    .rearrange("p a b -> p (a b)"),
                                    in_=bass.AP(V_d, par * 8388608 + h * 262144,
                                                [[2048, 128], [1, 2048]])
                                    ).then_inc(hload, 16)

            @block.vector
            def _(v):
                ser = serializer(v, vser2)
                for h in range(H):
                    if L0:
                        ser(v.tensor_scalar(alib[:], consts["iot"][:],
                                            float(SLOPES[h] / SCALE), None, OP.mult))
                        v.sem_inc(alibd, 1)
                    for pi, p in enumerate(positions):
                        nch = nseg_of(p)
                        if L0:
                            for seg in range(2):
                                for c in range(nch):
                                    i = sc_at[(h, pi, seg, c)]
                                    lo, hi = chunk_cols(seg, c)
                                    v.wait_ge(scd, i)
                                    if i >= 2:
                                        v.wait_ge(expd, i - 1)
                                    if c == nch - 1:
                                        ser(v.tensor_tensor(tf[i % 2][:], ps_sc[i % 2][:],
                                                            alib[:, lo:hi], OP.add))
                                        v.tensor_tensor(tf[i % 2][:], tf[i % 2][:],
                                                        consts["msk"][(p % 4) * 2 + seg][:],
                                                        OP.add).then_inc(alid, 1)
                                    else:
                                        v.tensor_tensor(tf[i % 2][:], ps_sc[i % 2][:],
                                                        alib[:, lo:hi],
                                                        OP.add).then_inc(alid, 1)
                        v.wait_ge(expd, sc_at[(h, pi, 1, nch - 1)])
                        ser(v.reduce_sum(rs[:, 8:9], rs[:, 0:2 * nch], axis=AX.X))
                        ser(v.reciprocal(rs[:, 9:10], rs[:, 8:9]))
                        for seg in range(2):
                            lo, hi = chunk_cols(seg, 0)[0], chunk_cols(seg, nch - 1)[1]
                            inst = v.tensor_scalar(P[:, lo:hi], P[:, lo:hi], rs[:, 9:10],
                                                   None, OP.mult)
                        inst.then_inc(nrmd, 1)
                        for seg in range(2):
                            for c in range(nch):
                                gi = tp_at[(h, pi, seg, c)]
                                v.wait_ge(tpd, gi)
                                v.tensor_copy(PT[:, 16 * seg + 4 * c:16 * seg + 4 * (c + 1), :],
                                              ps_pt[gi % 2][:].rearrange("p (a b) -> p a b", a=4)
                                              ).then_inc(ptd, 1)
                        v.wait_ge(avd, at_at[(h, pi)])
                        v.tensor_copy(attnT[:, h, 128 * pi:128 * (pi + 1)],
                                      ps_av[:]).then_inc(atd, 1)

            @block.scalar
            def _(s):
                for h in range(H):
                    for pi, p in enumerate(positions):
                        nch = nseg_of(p)
                        for seg in range(2):
                            for c in range(nch):
                                i = sc_at[(h, pi, seg, c)]
                                lo, hi = chunk_cols(seg, c)
                                if (h, pi) != (0, 0) and seg == 0 and c == 0:
                                    prev = (h, pi - 1) if pi > 0 else (h - 1, npos - 1)
                                    pp = positions[prev[1]]
                                    s.wait_ge(tpd, tp_at[(prev[0], prev[1], 1,
                                                          nseg_of(pp) - 1)])
                                if L0:
                                    s.wait_ge(alid, i)
                                    src_ap = tf[i % 2][:]
                                else:
                                    s.wait_ge(scd, i)
                                    src_ap = ps_sc[i % 2][:]
                                col = seg * nch + c
                                s.activation(P[:, lo:hi], src_ap, AF.Exp,
                                             bias=nsh[:, p, h:h + 1], scale=SCALE,
                                             accum_out=rs[:, col:col + 1]
                                             ).then_inc(expd, 1)

            @block.tensor
            def _(te):
                for h in range(H):
                    te.wait_ge(hload, 80 * (h + 1))
                    if L0:
                        te.wait_ge(alibd, h + 1)
                    for pi, p in enumerate(positions):
                        nch = nseg_of(p)
                        for seg in range(2):
                            for c in range(nch):
                                i = sc_at[(h, pi, seg, c)]
                                lo, hi = chunk_cols(seg, c)
                                if i >= 3:
                                    te.wait_ge(alid if L0 else expd, i - 2)
                                te.matmul(ps_sc[i % 2][:],
                                          qTh[h % 2][:, 128 * pi:128 * (pi + 1)],
                                          KTh[h % 2][:, lo:hi],
                                          start=True, stop=True).then_inc(scd, 1)
                        te.wait_ge(nrmd, nrm_at[(h, pi)])
                        for seg in range(2):
                            for c in range(nch):
                                gi = tp_at[(h, pi, seg, c)]
                                lo, _hi = chunk_cols(seg, c)
                                if gi >= 3:
                                    te.wait_ge(ptd, gi - 2)
                                inst = None
                                for jj in range(4):
                                    inst = te.transpose(
                                        ps_pt[gi % 2][:, 128 * jj:128 * (jj + 1)],
                                        P[:, lo + 128 * jj:lo + 128 * (jj + 1)], ident[:])
                                inst.then_inc(tpd, 1)
                        if at_at[(h, pi)] >= 2:
                            te.wait_ge(atd, at_at[(h, pi)] - 1)
                        first = True
                        for seg in range(2):
                            for c in range(nch):
                                te.wait_ge(ptd, tp_at[(h, pi, seg, c)])
                                for jj in range(4):
                                    kt = 16 * seg + 4 * c + jj
                                    inst = te.matmul(ps_av[:], Vh[h % 2][:, kt, :],
                                                     PT[:, kt, :],
                                                     start=first,
                                                     stop=(seg == 1 and c == nch - 1
                                                           and jj == 3))
                                    first = False
                        inst.then_inc(avd, 1)


def emit_ffn0(nc, sems, tag, lnT, tiles, vals, sc4b, sel4b, biasb, y_dram, CH=256, nch_limit=None):
    NCH = D // CH
    if nch_limit is not None:
        NCH = nch_limit
    nt = len(tiles)
    with contextlib.ExitStack() as c2:
        vf = [c2.enter_context(nc.sbuf_tensor(f"{tag}vf{i}", [128, 32, CH], BF16)) for i in range(2)]
        wde = [c2.enter_context(nc.sbuf_tensor(f"{tag}wde{i}", [128, 32, CH], BF16)) for i in range(2)]
        sctl = [c2.enter_context(nc.sbuf_tensor(f"{tag}sctl{i}", [128, 32, CH], BF16)) for i in range(2)]
        stage = [c2.enter_context(nc.sbuf_tensor(f"{tag}stg{i}", [128, CH], BF16)) for i in range(2)]
        ps = [c2.enter_context(nc.psum_tensor(f"{tag}cps{i}", [128, CH], F32)) for i in range(2)]
        ps_s = [c2.enter_context(nc.psum_tensor(f"{tag}csps{i}", [128, CH], F32)) for i in range(2)]
        vload, scmm, sevd, deqd, mmd, evd, outd, vser3 = sems(8)

        with nc.Block() as block:
            @block.gpsimd
            def _(g):
                for ch in range(NCH):
                    if ch >= 2:
                        g.wait_ge(deqd, ch - 1)
                    g.dma_start(out=vf[ch % 2][:, :16],
                                in_=vals.ap()[ch, :16].rearrange("a p c -> p a c")
                                ).then_inc(vload, 16)
                    g.dma_start(out=vf[ch % 2][:, 16:],
                                in_=vals.ap()[ch, 16:].rearrange("a p c -> p a c")
                                ).then_inc(vload, 16)

            @block.tensor
            def _(te):
                for ch in range(NCH):
                    for d in range(32):
                        sidx = ch * 32 + d
                        if sidx >= 2:
                            te.wait_ge(sevd, sidx - 1)
                        te.matmul(ps_s[sidx % 2][:], sel4b[:, d, :],
                                  sc4b[:, ch * CH:(ch + 1) * CH],
                                  start=True, stop=True).then_inc(scmm, 1)
                    te.wait_ge(deqd, ch + 1)
                    for ti in range(nt):
                        idx = ch * nt + ti
                        if idx >= 2:
                            te.wait_ge(evd, idx - 1)
                        pb = ps[idx % 2]
                        t = tiles[ti]
                        for d in range(32):
                            inst = te.matmul(pb[:], lnT[:, d, 128 * t:128 * (t + 1)],
                                             wde[ch % 2][:, d, :],
                                             start=(d == 0), stop=(d == 31))
                        inst.then_inc(mmd, 1)

            @block.vector
            def _(v):
                ser = serializer(v, vser3)
                for ch in range(NCH):
                    for d in range(32):
                        sidx = ch * 32 + d
                        v.wait_ge(scmm, sidx + 1)
                        inst = v.tensor_copy(sctl[ch % 2][:, d, :], ps_s[sidx % 2][:])
                        if d == 31:
                            ser(inst)
                            v.sem_inc(sevd, 1)
                        else:
                            inst.then_inc(sevd, 1)
                    v.wait_ge(vload, 32 * (ch + 1))
                    if ch >= 2:
                        v.wait_ge(mmd, nt * (ch - 1))
                    ser(v.tensor_scalar(vf[ch % 2][:].rearrange("p a b -> p (a b)"),
                                        vf[ch % 2][:].rearrange("p a b -> p (a b)"),
                                        -128.0, None, OP.add))
                    v.tensor_tensor(wde[ch % 2][:].rearrange("p a b -> p (a b)"),
                                    vf[ch % 2][:].rearrange("p a b -> p (a b)"),
                                    sctl[ch % 2][:].rearrange("p a b -> p (a b)"),
                                    OP.mult).then_inc(deqd, 1)
                    for ti in range(nt):
                        idx = ch * nt + ti
                        v.wait_ge(mmd, idx + 1)
                        if idx >= 2:
                            v.wait_ge(outd, 16 * (idx - 1))
                        v.tensor_tensor(stage[idx % 2][:], ps[idx % 2][:],
                                        biasb[:, ch * CH:(ch + 1) * CH],
                                        OP.add).then_inc(evd, 1)

            @block.sync
            def _(s):
                for ch in range(NCH):
                    for ti in range(nt):
                        idx = ch * nt + ti
                        s.wait_ge(evd, idx + 1)
                        s.dma_start(out=y_dram.ap()[ti, :, ch * CH:(ch + 1) * CH],
                                    in_=stage[idx % 2][:]).then_inc(outd, 16)
                s.wait_ge(outd, 16 * NCH * nt)


def emit_moe_prep(nc, sems, tag, lnT, tiles, wgtb, atb, ematb, bthrb, ident, gT):
    nt = len(tiles)
    with contextlib.ExitStack() as c2:
        gth = c2.enter_context(nc.sbuf_tensor(f"{tag}gth", [128, 16], F32))
        ge8 = c2.enter_context(nc.sbuf_tensor(f"{tag}ge8", [128, 8], F32))
        sst = c2.enter_context(nc.sbuf_tensor(f"{tag}sst", [128, 8], F32))
        hb = c2.enter_context(nc.sbuf_tensor(f"{tag}hb", [128, 128], BF16))
        hTb = c2.enter_context(nc.sbuf_tensor(f"{tag}hTb", [128, 128], BF16))
        w8 = c2.enter_context(nc.sbuf_tensor(f"{tag}w8", [128, 8], BF16))
        w8T = c2.enter_context(nc.sbuf_tensor(f"{tag}w8T", [8, 128], BF16))
        weT = c2.enter_context(nc.sbuf_tensor(f"{tag}weT", [128, 128], BF16))
        ps_g = c2.enter_context(nc.psum_tensor(f"{tag}ps_g", [128, 16], F32))
        ps_h = c2.enter_context(nc.psum_tensor(f"{tag}ps_h", [128, 128], F32))
        ps_t1 = c2.enter_context(nc.psum_tensor(f"{tag}ps_t1", [8, 128], BF16))
        ps_t2 = c2.enter_context(nc.psum_tensor(f"{tag}ps_t2", [128, 128], F32))
        ps_t3 = c2.enter_context(nc.psum_tensor(f"{tag}ps_t3", [128, 128], BF16))
        (seq,) = sems(1)

        chain = Chain()
        for ti in range(nt):
            t = tiles[ti]
            def mms(te, t=t):
                for d in range(32):
                    te.matmul(ps_g[:, :9], lnT[:, d, 128 * t:128 * (t + 1)],
                              wgtb[:, d, :9], start=(d == 0), stop=(d == 31))
                for d in range(32):
                    inst = te.matmul(ps_h[:], lnT[:, d, 128 * t:128 * (t + 1)],
                                     atb[:, d, :], start=(d == 0), stop=(d == 31))
                return inst
            chain.add("tensor", mms)
            chain.add("vector", lambda v: v.tensor_copy(gth[:, :9], ps_g[:, :9]))
            chain.add("vector", lambda v: v.reduce_max(sst[:, 0:1], gth[:, 0:8], axis=AX.X))
            chain.add("vector", lambda v: v.tensor_scalar(sst[:, 1:2], sst[:, 0:1],
                                                          -1.0, None, OP.mult))
            chain.add("scalar", lambda s: s.activation(ge8[:], gth[:, 0:8], AF.Exp,
                                                       bias=sst[:, 1:2],
                                                       accum_out=sst[:, 2:3]))
            chain.add("scalar", lambda s: s.activation(sst[:, 3:4], gth[:, 8:9],
                                                       AF.Sigmoid, bias=bthrb[:]))
            chain.add("vector", lambda v: v.reciprocal(sst[:, 4:5], sst[:, 2:3]))
            chain.add("vector", lambda v: v.tensor_scalar(ge8[:], ge8[:], sst[:, 4:5], None, OP.mult))
            chain.add("vector", lambda v: v.tensor_scalar(sst[:, 5:6], sst[:, 3:4], 0.125, None, OP.mult))
            chain.add("vector", lambda v: v.tensor_scalar(ge8[:], ge8[:], sst[:, 5:6], None, OP.subtract))
            chain.add("vector", lambda v: v.tensor_scalar(ge8[:], ge8[:], 0.0, None, OP.max))
            chain.add("vector", lambda v: v.reduce_sum(sst[:, 6:7], ge8[:], axis=AX.X))
            chain.add("vector", lambda v: v.tensor_scalar(sst[:, 6:7], sst[:, 6:7], 1e-30, None, OP.max))
            chain.add("vector", lambda v: v.reciprocal(sst[:, 7:8], sst[:, 6:7]))
            chain.add("vector", lambda v: v.tensor_scalar(w8[:], ge8[:], sst[:, 7:8], 2.0, OP.mult, OP.mult))
            chain.add("tensor", lambda te: te.transpose(ps_t1[:], w8[:], ident[:]))
            chain.add("vector", lambda v: v.tensor_copy(w8T[:], ps_t1[:]))
            chain.add("tensor", lambda te: te.matmul(ps_t2[:], ematb[:], w8T[:],
                                                     start=True, stop=True))
            chain.add("vector", lambda v: v.tensor_copy(weT[:], ps_t2[:]))
            chain.add("vector", lambda v: v.tensor_copy(hb[:], ps_h[:]))
            chain.add("tensor", lambda te: te.transpose(ps_t3[:], hb[:], ident[:]))
            chain.add("vector", lambda v: v.tensor_copy(hTb[:], ps_t3[:]))
            chain.add("vector",
                      lambda v, ti=ti: v.tensor_tensor(gT[:, ti, :], hTb[:], weT[:], OP.mult))
        chain.emit(nc, seq)



def emit_dump(nc, sems, pairs):
    (dsem,) = sems(1)
    with nc.Block() as block:
        @block.sync
        def _(s):
            n = 0
            for out_t, in_t in pairs:
                s.dma_start(out=out_t.ap(), in_=in_t.ap()).then_inc(dsem, 16)
                n += 16
            s.wait_ge(dsem, n)


@contextlib.contextmanager
def phase_group(nc):
    with nc.reset_on_exit():
        with contextlib.ExitStack() as gctx:
            yield make_sem_factory(nc, gctx)


def build_F(stop=None):
    import os
    nc = bass.Bass("TRN2", target_bir_lowering=False)
    x16 = nc.dram_tensor("x16", [NTL, 128, D], F32, kind="ExternalInput")
    wt0 = nc.dram_tensor("wt0", [48, 32, 128, 256], F32, kind="ExternalInput")
    lnw = {nm: nc.dram_tensor(nm, [128, D], F32, kind="ExternalInput")
           for nm in ("qw", "qb", "kw", "kb")}
    wo0 = nc.dram_tensor("wo0", [16, 32, 128, 256], F32, kind="ExternalInput")
    vals = nc.dram_tensor("vals", [32, 32, 128, 128], I32, kind="ExternalInput")
    sc4 = nc.dram_tensor("sc4", [128, D], F32, kind="ExternalInput")
    sel4 = nc.dram_tensor("sel4", [128, 32 * 128], BF16, kind="ExternalInput")
    bias0_d = nc.dram_tensor("bias0", [128, D], F32, kind="ExternalInput")
    wkv1 = nc.dram_tensor("wkv1", [32, 32, 128, 256], F32, kind="ExternalInput")
    wq1 = nc.dram_tensor("wq1", [16, 32, 128, 256], F32, kind="ExternalInput")
    wo1 = nc.dram_tensor("wo1", [16, 32, 128, 256], F32, kind="ExternalInput")
    wb_t = nc.dram_tensor("wb_t", [16, 32, 128, 256], F32, kind="ExternalInput")
    wgt = nc.dram_tensor("wgt", [128, 32, 16], F32, kind="ExternalInput")
    at_t = nc.dram_tensor("at_t", [128, 32, 128], F32, kind="ExternalInput")
    bf_t = nc.dram_tensor("bf_t", [128, D], F32, kind="ExternalInput")
    emat = nc.dram_tensor("emat", [8, 128], BF16, kind="ExternalInput")
    bias1_d = nc.dram_tensor("bias1", [128, D], F32, kind="ExternalInput")
    bthr = nc.dram_tensor("bthr", [128, 1], F32, kind="ExternalInput")
    ident_d = nc.dram_tensor("ident", [128, 128], BF16, kind="ExternalInput")
    iota_d = nc.dram_tensor("iota", [128, 4096], F32, kind="ExternalInput")
    mask_d = nc.dram_tensor("maskneg", [8, 128, 512], BF16, kind="ExternalInput")
    qpos_d = nc.dram_tensor("qpos", [128, NTL], F32, kind="ExternalInput")
    slopes_d = nc.dram_tensor("slopes", [128, H], F32, kind="ExternalInput")
    offs_d = nc.dram_tensor("offs", [1, 8], I32, kind="ExternalInput")
    eps_d = nc.dram_tensor("eps_in", [128, 1], F32, kind="ExternalInput")
    ones_d = nc.dram_tensor("ones_in", [1, 128], F32, kind="ExternalInput")
    out_o = nc.dram_tensor("out_o", [NTO, 128, D], F32, kind="ExternalOutput")

    q_nat0 = nc.dram_tensor("q_nat0", [NTL, 128, D], BF16)
    k_nat0 = nc.dram_tensor("k_nat0", [NTL, 128, D], BF16)
    v_nat0 = nc.dram_tensor("v_nat0", [NTL, 128, D], BF16)
    qT0 = nc.dram_tensor("qT0", [H, 128, NTL * 128], BF16)
    qn2_0 = nc.dram_tensor("qn2_0", [128, NTL * H], F32)
    y0 = nc.dram_tensor("y0", [NTL, 128, D], BF16)
    x1 = nc.dram_tensor("x1", [NTL, 128, D], F32)
    x2 = nc.dram_tensor("x2", [NTL, 128, D], F32)
    x2out = nc.dram_tensor("x2out", [NTO, 128, D], F32)
    q_nat1 = nc.dram_tensor("q_nat1", [NTO, 128, D], BF16)
    k_nat1 = nc.dram_tensor("k_nat1", [NTL, 128, D], BF16)
    v_nat1 = nc.dram_tensor("v_nat1", [NTL, 128, D], BF16)
    qT1 = nc.dram_tensor("qT1", [H, 128, NTO * 128], BF16)
    qn2_1 = nc.dram_tensor("qn2_1", [128, NTO * H], F32)
    y1 = nc.dram_tensor("y1", [NTO, 128, D], BF16)
    x3 = nc.dram_tensor("x3", [NTO, 128, D], F32)
    y2 = nc.dram_tensor("y2", [NTO, 128, D], BF16)
    v_stg = nc.dram_tensor("v_stg", [H, 128, 16, 128], BF16)

    kT0_sh = nc.dram_tensor("kT0_sh", [2, H, 128, 2048], BF16, addr_space="Shared")
    v0_sh = nc.dram_tensor("v0_sh", [2, H, 128, 16, 128], BF16, addr_space="Shared")
    km0_sh = nc.dram_tensor("km0_sh", [2, H], F32, addr_space="Shared")
    kT1_sh = nc.dram_tensor("kT1_sh", [2, H, 128, 2048], BF16, addr_space="Shared")
    v1_sh = nc.dram_tensor("v1_sh", [2, H, 128, 16, 128], BF16, addr_space="Shared")
    km1_sh = nc.dram_tensor("km1_sh", [2, H], F32, addr_space="Shared")
    flags0 = nc.dram_tensor("flags0", [2, 16], I32, addr_space="Shared")
    flags1 = nc.dram_tensor("flags1", [2, 16], I32, addr_space="Shared")

    nseg_l0 = lambda p: (p + 4) // 4
    nseg_l1 = lambda p: 4

    with contextlib.ExitStack() as ctx:
        ident = ctx.enter_context(nc.sbuf_tensor("identt", [128, 128], BF16))
        epst = ctx.enter_context(nc.sbuf_tensor("epst", [128, 1], F32))
        onesb = ctx.enter_context(nc.sbuf_tensor("onesb", [1, 128], F32))
        myoff = ctx.enter_context(nc.sbuf_tensor("myoff", [1, 8], I32))
        scri = ctx.enter_context(nc.sbuf_tensor("scri", [1, 64], I32))
        nsh0 = ctx.enter_context(nc.sbuf_tensor("nsh0", [128, NTL, H], F32))
        nsh1 = ctx.enter_context(nc.sbuf_tensor("nsh1", [128, NTO, H], F32))

        with phase_group(nc) as sems:
            (ld0,) = sems(1)
            with nc.Block() as block:
                @block.sync
                def _(s):
                    s.dma_start(out=ident[:], in_=ident_d.ap()).then_inc(ld0, 16)
                    s.dma_start(out=myoff[:], in_=offs_d.ap()).then_inc(ld0, 16)
                    s.dma_start(out=epst[:], in_=eps_d.ap()).then_inc(ld0, 16)
                    s.dma_start(out=onesb[:], in_=ones_d.ap()).then_inc(ld0, 16)

                @block.gpsimd
                def _(g):
                    g.wait_ge(ld0, 64)

        with phase_group(nc) as sems:
            with contextlib.ExitStack() as cA:
                lnT = cA.enter_context(nc.sbuf_tensor("lnTa", [128, 32, NTL * 128], BF16))
                emit_ln_lnT(nc, sems, "l0a", [x16.ap()[t] for t in range(NTL)], lnT,
                            ident, epst, nt=NTL)

                def dst0(ch, ti):
                    proj = ch // 16
                    fc = (ch % 16) * 256
                    dstn = (q_nat0, k_nat0, v_nat0)[proj]
                    return dstn.ap()[ti, :, fc:fc + 256]

                emit_proj_phase(nc, sems, "p0", lnT, wt0, 48, 256, dst0, clip=True,
                                tiles=list(range(NTL)))
            if stop == "proj0":
                d1 = nc.dram_tensor("dbg_q", [NTL, 128, D], BF16, kind="ExternalOutput")
                d2 = nc.dram_tensor("dbg_k", [NTL, 128, D], BF16, kind="ExternalOutput")
                d3 = nc.dram_tensor("dbg_v", [NTL, 128, D], BF16, kind="ExternalOutput")
                emit_dump(nc, sems, [(d1, q_nat0), (d2, k_nat0), (d3, v_nat0)])
        if stop == "proj0":
            return nc

        with phase_group(nc) as sems:
            for which in (0, 1):
                with contextlib.ExitStack() as cB:
                    wq = cB.enter_context(nc.sbuf_tensor(f"wqv{which}", [128, D], BF16))
                    bq = cB.enter_context(nc.sbuf_tensor(f"bqv{which}", [128, D], BF16))
                    (ld1,) = sems(1)
                    keys = ("qw", "qb") if which == 0 else ("kw", "kb")
                    with nc.Block() as block:
                        @block.gpsimd
                        def _(g):
                            g.dma_start(out=wq[:], in_=lnw[keys[0]].ap()).then_inc(ld1, 16)
                            g.dma_start(out=bq[:], in_=lnw[keys[1]].ap()).then_inc(ld1, 16)
                            g.wait_ge(ld1, 32)
                    if which == 0:
                        emit_p2b_pass(nc, sems, "p2q0", q_nat0, qT0, ident, True, wq[:], bq[:],
                                      "q", qn2_o=qn2_0, epst=epst, nt=NTL)
                    else:
                        emit_p2b_pass(nc, sems, "p2k0", k_nat0, None, ident, True, wq[:], bq[:],
                                      "ksh", epst=epst, nt=NTL,
                                      shared_out=dict(kT_sh=kT0_sh, km_sh=km0_sh, myoff_sb=myoff))
            emit_vshuffle(nc, sems, "vs0", v_nat0, v_stg, v0_sh, myoff, NTL)
            emit_barrier(nc, sems, "bar0", flags0, scri, myoff, 3)
            if stop == "bar0":
                d1 = nc.dram_tensor("dbg_kT", [2, H, 128, 2048], BF16, kind="ExternalOutput")
                d2 = nc.dram_tensor("dbg_vsh", [2, H, 128, 16 * 128], BF16, kind="ExternalOutput")
                d3 = nc.dram_tensor("dbg_km", [2, H], F32, kind="ExternalOutput")
                d4 = nc.dram_tensor("dbg_qT", [H, 128, NTL * 128], BF16, kind="ExternalOutput")
                d5 = nc.dram_tensor("dbg_qn2", [128, NTL * H], F32, kind="ExternalOutput")
                emit_dump(nc, sems, [(d1, kT0_sh), (d2, v0_sh), (d3, km0_sh), (d4, qT0),
                                     (d5, qn2_0)])
        if stop == "bar0":
            return nc

        with phase_group(nc) as sems:
            with contextlib.ExitStack() as cATT:
                iot = cATT.enter_context(nc.sbuf_tensor("iot", [128, 4096], F32))
                msk = [cATT.enter_context(nc.sbuf_tensor(f"msk{i}", [128, 512], BF16)) for i in range(8)]
                qps = cATT.enter_context(nc.sbuf_tensor("qps", [128, NTL], F32))
                slp = cATT.enter_context(nc.sbuf_tensor("slp", [128, H], F32))
                (ldc,) = sems(1)
                with nc.Block() as block:
                    @block.sync
                    def _(s):
                        s.dma_start(out=iot[:], in_=iota_d.ap()).then_inc(ldc, 16)
                        for i in range(8):
                            s.dma_start(out=msk[i][:], in_=mask_d.ap()[i]).then_inc(ldc, 16)
                        s.dma_start(out=qps[:], in_=qpos_d.ap()).then_inc(ldc, 16)
                        s.dma_start(out=slp[:], in_=slopes_d.ap()).then_inc(ldc, 16)
                        s.wait_ge(ldc, 176)
                consts = dict(iot=iot, msk=msk, qps=qps, slp=slp, ones=onesb)
                emit_shift_prep(nc, sems, "sh0", 0, NTL, qn2_0, km0_sh, nsh0, consts, epst)
                for half in range(2):
                    with contextlib.ExitStack() as cH:
                        attnT = cH.enter_context(nc.sbuf_tensor(f"attnT0{half}",
                                                                [128, 32, NTO * 128], BF16))
                        positions = list(range(half * 8, half * 8 + 8))
                        emit_attention(nc, sems, f"at0{half}", 0, positions, nseg_l0,
                                       qT0, half * 1024, kT0_sh, v0_sh, nsh0, attnT,
                                       ident, consts)

                        def dsty0(ch, ti, half=half):
                            return y0.ap()[half * 8 + ti, :, ch * 256:(ch + 1) * 256]

                        emit_proj_phase(nc, sems, f"op0{half}", attnT, wo0, 16, 256,
                                        dsty0, clip=False, tiles=list(range(8)))

        with phase_group(nc) as sems:
            emit_ln_residual(nc, sems, "lr0", [y0.ap()[t] for t in range(NTL)],
                             [x16.ap()[t] for t in range(NTL)],
                             [x1.ap()[t] for t in range(NTL)], epst=epst, nt=NTL)
            if stop == "x1":
                d1 = nc.dram_tensor("dbg_x1", [NTL, 128, D], F32, kind="ExternalOutput")
                emit_dump(nc, sems, [(d1, x1)])
        if stop == "x1":
            return nc

        with phase_group(nc) as sems:
            with contextlib.ExitStack() as cC:
                sc4b = cC.enter_context(nc.sbuf_tensor("sc4b", [128, D], BF16))
                sel4b = cC.enter_context(nc.sbuf_tensor("sel4b", [128, 32, 128], BF16))
                biasb0 = cC.enter_context(nc.sbuf_tensor("biasb0", [128, D], BF16))
                (ld2,) = sems(1)
                with nc.Block() as block:
                    @block.gpsimd
                    def _(g):
                        g.dma_start(out=sc4b[:], in_=sc4.ap()).then_inc(ld2, 16)
                        g.dma_start(out=sel4b[:].rearrange("p a b -> p (a b)"),
                                    in_=sel4.ap()).then_inc(ld2, 16)
                        g.dma_start(out=biasb0[:], in_=bias0_d.ap()).then_inc(ld2, 16)
                        g.wait_ge(ld2, 48)
                lnT = cC.enter_context(nc.sbuf_tensor("lnTc", [128, 32, NTL * 128], BF16))
                emit_ln_lnT(nc, sems, "l0f", [x1.ap()[t] for t in range(NTL)], lnT,
                            ident, epst, nt=NTL)
                emit_ffn0(nc, sems, "f0", lnT, list(range(NTL)), vals, sc4b, sel4b,
                          biasb0, y0, CH=128,
                          nch_limit=32)
            emit_ln_residual(nc, sems, "lr1", [y0.ap()[t] for t in range(NTL)],
                             [x1.ap()[t] for t in range(NTL)],
                             [x2.ap()[t] for t in range(NTL)], epst=epst, nt=NTL)
            if stop == "x2":
                d1 = nc.dram_tensor("dbg_x2", [NTL, 128, D], F32, kind="ExternalOutput")
                emit_dump(nc, sems, [(d1, x2)])
        if stop == "x2":
            return nc

        with phase_group(nc) as sems:
            (gx,) = sems(1)
            with nc.Block() as block:
                @block.gpsimd
                def _(g):
                    with g.register("xob") as xob:
                        g.reg_load(xob, myoff[:, 5:6])
                        g.dma_start(
                            out=x2out.ap().rearrange("k p d -> p k d"),
                            in_=bass.AP(x2, xob, [[4096, 128], [1048576, 8], [1, 4096]])
                            ).then_inc(gx, 16)
                        g.wait_ge(gx, 16)

            with contextlib.ExitStack() as cD:
                lnT = cD.enter_context(nc.sbuf_tensor("lnTd", [128, 32, NTL * 128], BF16))
                emit_ln_lnT(nc, sems, "l1a", [x2.ap()[t] for t in range(NTL)], lnT,
                            ident, epst, nt=NTL)

                def dstkv1(ch, ti):
                    proj = ch // 16
                    fc = (ch % 16) * 256
                    dstn = (k_nat1, v_nat1)[proj]
                    return dstn.ap()[ti, :, fc:fc + 256]

                emit_proj_phase(nc, sems, "p1kv", lnT, wkv1, 32, 256, dstkv1,
                                clip=False, tiles=list(range(NTL)))

            with contextlib.ExitStack() as cDq:
                lnT8q = cDq.enter_context(nc.sbuf_tensor("lnT8q", [128, 32, NTO * 128], BF16))
                emit_ln_lnT(nc, sems, "l1q", [x2out.ap()[t] for t in range(NTO)], lnT8q,
                            ident, epst, nt=NTO)

                def dstq1(ch, ti):
                    return q_nat1.ap()[ti, :, ch * 256:(ch + 1) * 256]

                emit_proj_phase(nc, sems, "p1q", lnT8q, wq1, 16, 256, dstq1,
                                clip=False, tiles=list(range(NTO)))

        with phase_group(nc) as sems:
            emit_p2b_pass(nc, sems, "p2q1", q_nat1, qT1, ident, False, None, None,
                          "q", qn2_o=qn2_1, epst=epst, nt=NTO)
            emit_p2b_pass(nc, sems, "p2k1", k_nat1, None, ident, False, None, None,
                          "ksh", epst=epst, nt=NTL,
                          shared_out=dict(kT_sh=kT1_sh, km_sh=km1_sh, myoff_sb=myoff))
            emit_vshuffle(nc, sems, "vs1", v_nat1, v_stg, v1_sh, myoff, NTL)
            emit_barrier(nc, sems, "bar1", flags1, scri, myoff, 4)

        with phase_group(nc) as sems:
            emit_shift_prep(nc, sems, "sh1", 1, NTO, qn2_1, km1_sh, nsh1,
                            dict(ones=onesb), epst)
            with contextlib.ExitStack() as cH:
                attnT = cH.enter_context(nc.sbuf_tensor("attnT1", [128, 32, NTO * 128], BF16))
                emit_attention(nc, sems, "at1", 1, list(range(NTO)), nseg_l1,
                               qT1, 0, kT1_sh, v1_sh, nsh1, attnT, ident, {})

                def dsty1(ch, ti):
                    return y1.ap()[ti, :, ch * 256:(ch + 1) * 256]

                emit_proj_phase(nc, sems, "op1", attnT, wo1, 16, 256, dsty1,
                                clip=False, tiles=list(range(NTO)))
            emit_ln_residual(nc, sems, "lr2", [y1.ap()[t] for t in range(NTO)],
                             [x2out.ap()[t] for t in range(NTO)],
                             [x3.ap()[t] for t in range(NTO)], epst=epst, nt=NTO)
            if stop == "x3":
                d1 = nc.dram_tensor("dbg_x3", [NTO, 128, D], F32, kind="ExternalOutput")
                emit_dump(nc, sems, [(d1, x3)])
        if stop == "x3":
            return nc

        with phase_group(nc) as sems:
            with contextlib.ExitStack() as cE:
                wgtb = cE.enter_context(nc.sbuf_tensor("wgtb", [128, 32, 16], BF16))
                atb = cE.enter_context(nc.sbuf_tensor("atb", [128, 32, 128], BF16))
                ematb = cE.enter_context(nc.sbuf_tensor("ematb", [8, 128], BF16))
                bthrb = cE.enter_context(nc.sbuf_tensor("bthrb", [128, 1], F32))
                bfb = cE.enter_context(nc.sbuf_tensor("bfb", [128, D], BF16))
                biasb1 = cE.enter_context(nc.sbuf_tensor("biasb1", [128, D], BF16))
                gT = cE.enter_context(nc.sbuf_tensor("gT", [128, NTO, 128], BF16))
                (ld3,) = sems(1)
                with nc.Block() as block:
                    @block.sync
                    def _(s):
                        s.dma_start(out=ematb[:], in_=emat.ap()).then_inc(ld3, 16)
                        s.dma_start(out=bthrb[:], in_=bthr.ap()).then_inc(ld3, 16)

                    @block.gpsimd
                    def _(g):
                        g.dma_start(out=biasb1[:], in_=bias1_d.ap()).then_inc(ld3, 16)
                        g.dma_start(out=wgtb[:].rearrange("p a b -> p (a b)"),
                                    in_=wgt.ap().rearrange("p a b -> p (a b)")).then_inc(ld3, 16)
                        g.dma_start(out=atb[:].rearrange("p a b -> p (a b)"),
                                    in_=at_t.ap().rearrange("p a b -> p (a b)")).then_inc(ld3, 16)
                        g.dma_start(out=bfb[:], in_=bf_t.ap()).then_inc(ld3, 16)
                        g.wait_ge(ld3, 96)

                lnT8 = cE.enter_context(nc.sbuf_tensor("lnTe", [128, 32, NTO * 128], BF16))
                emit_ln_lnT(nc, sems, "l1f", [x3.ap()[t] for t in range(NTO)], lnT8,
                            ident, epst, nt=NTO)
                emit_moe_prep(nc, sems, "moe", lnT8, list(range(NTO)), wgtb, atb,
                              ematb, bthrb, ident, gT)

                def extra_mm(te, ch, ti, pb):
                    return te.matmul(pb[:], gT[:, ti, :], bfb[:, ch * 256:(ch + 1) * 256],
                                     start=False, stop=True)

                def evac_extra(v, ch, ti, stg):
                    return v.tensor_tensor(stg[:], stg[:], biasb1[:, ch * 256:(ch + 1) * 256],
                                           OP.add)

                def dsty2(ch, ti):
                    return y2.ap()[ti, :, ch * 256:(ch + 1) * 256]

                emit_proj_phase(nc, sems, "bp", lnT8, wb_t, 16, 256, dsty2, clip=False,
                                extra_mm=extra_mm, evac_extra=evac_extra,
                                tiles=list(range(NTO)))
            emit_ln_residual(nc, sems, "lr3", [y2.ap()[t] for t in range(NTO)],
                             [x3.ap()[t] for t in range(NTO)],
                             [out_o.ap()[t] for t in range(NTO)], epst=epst, nt=NTO)
    return nc


"""Host orchestration for the fused single-launch kernel."""
import numpy as np
import ml_dtypes

NPF32 = np.float32

def tile_w(wT, CH):
    NF = wT.shape[1]
    return np.ascontiguousarray(wT.reshape(32, 128, NF // CH, CH).transpose(2, 0, 1, 3))

def bcast(v):
    return np.ascontiguousarray(np.broadcast_to(np.asarray(v, NPF32), (128, len(v))))

def core_geom(c):
    j = c % 4
    lo = j % 2
    hi = lo + 2
    base = j // 2
    tiles = []
    for m in range(8):
        tiles.append(4 * m + lo)
        tiles.append(4 * m + hi)
    return c // 4, j, lo, hi, base, tiles

IDENT = np.eye(128, dtype=BF)
SEL4 = None
_b = np.arange(128)[:, None]
_dtp = np.arange(32 * 128)[None, :]
SEL4 = np.ascontiguousarray((_b == 4 * (_dtp // 128) + (_dtp % 128) // 32).astype(BF))
EMAT = np.ascontiguousarray((np.arange(128)[None, :] // 16 == np.arange(8)[:, None]).astype(BF))
SLOPES_B = np.ascontiguousarray(np.broadcast_to(SLOPES.astype(NPF32), (128, H)))

_col = np.arange(4096)
_par = _col // 2048
_jj = (_col % 2048) // 128
_ii = _col % 128
KPOS = ((2 * _jj + _par) * 128 + _ii).astype(NPF32)
KPOS_B = np.ascontiguousarray(np.broadcast_to(KPOS, (128, 4096)))

def masks_for(lo):
    out = np.zeros((8, 128, 512), NPF32)
    i = np.arange(128)[:, None]
    cc = np.arange(512)[None, :]
    for d in range(4):
        delta = lo + 2 * d
        for s in range(2):
            krel = 256 * (cc // 128) + s * 128 + (cc % 128)
            vis = krel <= delta * 128 + i
            out[2 * d + s] = np.where(vis, 0.0, -1.0e30)
    return np.ascontiguousarray(out.astype(BF))

def qpos_for(c):
    _, _, lo, hi, _, tiles = core_geom(c)
    i = np.arange(128)
    return np.ascontiguousarray(
        np.stack([t * 128 + i for t in tiles], axis=1).astype(NPF32))

_NONCE = [1]

_BUILT = {}
def get_ncF(stop=None):
    if stop not in _BUILT:
        _BUILT[stop] = build_F(stop=stop)
    return _BUILT[stop]


def run_fused(inp, stop=None, raw=False):
    x_full = np.asarray(inp["L0_attn_hidden_states"], NPF32)
    wqkvT = np.ascontiguousarray(np.asarray(inp["L0_attn_w_qkv"], NPF32).T)
    wt0 = tile_w(wqkvT, 256)
    lnw = dict(qw=bcast(inp["L0_attn_q_ln_w"]), qb=bcast(inp["L0_attn_q_ln_b"]),
               kw=bcast(inp["L0_attn_k_ln_w"]), kb=bcast(inp["L0_attn_k_ln_b"]))
    wo0 = tile_w(np.ascontiguousarray(np.asarray(inp["L0_attn_w_out"], NPF32).T), 256)
    valsT = np.ascontiguousarray(np.asarray(inp["L0_ffn_q_vals"], np.int32)
                                 .reshape(4096, 4096).T)
    vals = np.ascontiguousarray(valsT.reshape(32, 128, 32, 128).transpose(2, 0, 1, 3))
    sc4 = np.ascontiguousarray(np.asarray(inp["L0_ffn_q_scales"], NPF32).T)
    bias0 = bcast(inp["L0_ffn_bias"])
    wkv1 = tile_w(np.ascontiguousarray(
        np.concatenate([np.asarray(inp["L1_attn_w_k"], NPF32).T,
                        np.asarray(inp["L1_attn_w_v"], NPF32).T], axis=1)), 256)
    wq1 = tile_w(np.ascontiguousarray(np.asarray(inp["L1_attn_w_q"], NPF32).T), 256)
    wo1 = tile_w(np.ascontiguousarray(np.asarray(inp["L1_attn_w_o"], NPF32).T), 256)
    wb_t = tile_w(np.ascontiguousarray(np.asarray(inp["L1_ffn_W_base"], NPF32).T), 256)
    G = np.concatenate([np.asarray(inp["L1_ffn_W_gate"], NPF32).T,
                        np.asarray(inp["L1_ffn_W_thr"], NPF32).T,
                        np.zeros((4096, 7), NPF32)], axis=1)
    wgt = np.ascontiguousarray(G.reshape(32, 128, 16).transpose(1, 0, 2))
    AT = np.asarray(inp["L1_ffn_A"], NPF32).reshape(128, 4096).T
    at_t = np.ascontiguousarray(AT.reshape(32, 128, 128).transpose(1, 0, 2))
    bf_t = np.ascontiguousarray(np.asarray(inp["L1_ffn_B"], NPF32)
                                .transpose(0, 2, 1).reshape(128, 4096))
    bias1 = bcast(inp["L1_ffn_b_base"])
    bthr = np.ascontiguousarray(
        np.broadcast_to(np.asarray(inp["L1_ffn_b_thr"], NPF32), (128, 1)))

    nonce0 = _NONCE[0]
    nonce1 = nonce0 + 1
    _NONCE[0] += 2

    maps = []
    for c in range(8):
        b, j, lo, hi, base, tiles = core_geom(c)
        x16 = np.ascontiguousarray(x_full[b].reshape(32, 128, D)[tiles])
        offs = np.array([[lo * 16, lo * 8388608, lo * 32, nonce0, nonce1,
                          base * 524288, 0, 0]], np.int32)
        m = dict(x16=x16, wt0=wt0, wo0=wo0, vals=vals, sc4=sc4, sel4=SEL4,
                 bias0=bias0, wkv1=wkv1, wq1=wq1, wo1=wo1, wb_t=wb_t, wgt=wgt,
                 at_t=at_t, bf_t=bf_t, emat=EMAT, bias1=bias1, bthr=bthr,
                 ident=IDENT, iota=KPOS_B, maskneg=masks_for(lo),
                 qpos=qpos_for(c), slopes=SLOPES_B, offs=offs,
                 eps_in=np.full((128, 1), 1e-5, NPF32),
                 ones_in=np.ones((1, 128), NPF32), **lnw)
        maps.append(m)
    nc = get_ncF(stop)
    res = run_bass_kernel_spmd(nc, maps, core_ids=list(range(8)))
    if raw or stop is not None:
        return res
    full = np.zeros((2, 32, 128, D), NPF32)
    for c in range(8):
        b, j, lo, hi, base, tiles = core_geom(c)
        out_tiles = [4 * m + j for m in range(8)]
        full[b, out_tiles] = res.results[c]["out_o"]
    return full.reshape(2, 4096, D)


def kernel(**inputs):
    return run_fused(inputs)
